# version 17
# baseline (speedup 1.0000x reference)
import numpy as np
import ml_dtypes
import concourse.bass as bass
import concourse.mybir as mybir
from concourse.bass_utils import run_bass_kernel_spmd
from contextlib import ExitStack

F32 = mybir.dt.float32
BF16 = mybir.dt.bfloat16
OP = mybir.AluOpType
AF = mybir.ActivationFunctionType

NCORE = 8
BFULL, L, D = 16, 2048, 1024
BLOC = BFULL // NCORE
DI, H, DS, HD = 2048, 4, 64, 512
DP, FF = 4612, 4096
TG = 256
SPQ = 512
SPY = 160
NG = L // TG
Q = 128
NCH = TG // Q
NKT = D // 128
RMS_EPS = 1.1920929e-07
LN_EPS = 1e-5
ZOFF, XOFF, DTOFF, BOFF, COFF = 0, 2048, 4096, 4100, 4356

_PWL_Y_BITS = [
    0xb904c000, 0xb9adbb00, 0xba556b00, 0xbaf5ce80, 0xbb84b340, 0xbc064180,
    0xbc7e5920, 0xbce15244, 0xbd3a5e80, 0xbd8f926e, 0xbdcd3b48, 0xbe073b54,
    0xbe227686, 0xbe2e0cbe, 0xbe1df8a2, 0xbdcd765d, 0x00000000, 0x3e1944d2,
    0x3eb103af, 0x3f147cd0, 0x3f57625e, 0x3f8f1895, 0x3fb32c4c, 0x3fd706d9,
    0x3ffa2d0c, 0x400e3d5c, 0x401f01a7, 0x402f79be, 0x403fbda6, 0x404fe146,
    0x405ff2aa, 0x406ffa93, 0x407ffded,
]
PWL_Y = np.array(_PWL_Y_BITS, dtype=np.uint32).view(np.float32)
YP64 = PWL_Y.astype(np.float64) * 256.0
_SL = [(YP64[j + 1] - YP64[j]) / 64.0 for j in range(32)]
GAM = [float(np.float32(_SL[0]))] + \
      [float(np.float32(_SL[j] - _SL[j - 1])) for j in range(1, 32)] + \
      [float(np.float32(1.0 - _SL[31]))]
NSEG = 33
Y0 = float(np.float32(YP64[0]))
BASE2 = 1024.5

ENGS = ("sync", "gpsimd", "tensor", "vector", "scalar")
SEMS = ("ld", "st", "mm", "vec", "act")


def tmax(*tokens):
    best = {}

    def _add(t):
        if not t:
            return
        if isinstance(t, tuple) and len(t) == 2 and isinstance(t[0], str):
            s, v = t
            if v > best.get(s, 0):
                best[s] = v
            return
        for x in t:
            _add(x)

    for t in tokens:
        _add(t)
    return [(s, v) for s, v in best.items()]


class Sched:

    def __init__(self):
        self.items = {e: [] for e in ENGS}
        self.count = {s: 0 for s in SEMS}
        self._barrier = {e: None for e in ENGS}
        self._cons = {}

    def snap(self):
        return dict(self.count)

    def barrier_all(self):
        snap = [(s, self.count[s]) for s in SEMS if self.count[s] > 0]
        for e in ENGS:
            self._barrier[e] = snap

    def op(self, eng, method, *args, waits=(), inc=None, amt=1, **kwargs):
        w = tmax(list(waits) if not isinstance(waits, list) else waits)
        if method == "dma_start" and inc in ("ld", "st"):
            w = tmax(w, [(inc, self.count[inc])])
        if self._barrier[eng]:
            w = tmax(w, self._barrier[eng])
            self._barrier[eng] = None
        self.items[eng].append((tuple(w), method, args, kwargs, inc, amt))
        if inc is not None:
            self.count[inc] += amt
            return (inc, self.count[inc])
        return None

    def wait_only(self, eng, waits):
        self.items[eng].append((tuple(tmax(list(waits))), None, (), {}, None, 0))

    def use(self, name, tok):
        if tok:
            self._cons.setdefault(name, []).append(tok)

    def guard(self, name):
        toks = self._cons.pop(name, [])
        return tmax(*toks)

    OWN = {"tensor": "mm", "vector": "vec", "scalar": "act"}

    def _targets(self):
        if getattr(self, "_tg", None) is not None:
            return self._tg
        tg = {s: set() for s in SEMS}
        for eng_name in ENGS:
            own = self.OWN.get(eng_name)
            self_inc = 0
            hw = {}
            for waits, method, args, kwargs, inc, amt in self.items[eng_name]:
                for (s, v) in waits:
                    if s == own and v <= self_inc:
                        continue
                    if v > hw.get(s, 0):
                        tg[s].add(v)
                        hw[s] = v
                if inc == own:
                    self_inc += amt
        self._rank = {}
        for s in ("mm", "vec", "act"):
            self._rank[s] = {v: i + 1 for i, v in enumerate(sorted(tg[s]))}
        self._tg = tg
        return tg

    def replay(self, eng_name, eng, sems):
        tg = self._targets()
        own = self.OWN.get(eng_name)
        self_inc = 0
        hw = {}
        for waits, method, args, kwargs, inc, amt in self.items[eng_name]:
            for (s, v) in waits:
                if s == own and v <= self_inc:
                    continue
                if v > hw.get(s, 0):
                    eng.wait_ge(sems[s], self._rank[s][v] if s in self._rank
                                else v)
                    hw[s] = v
            if method is None:
                if inc == own:
                    self_inc += amt
                continue
            try:
                instr = getattr(eng, method)(*args, **kwargs)
            except Exception:
                import sys
                print(f"REPLAY FAIL {eng_name} {method} args={args} kwargs={list(kwargs.items())[:4]}",
                      file=sys.stderr)
                raise
            if inc is not None:
                if inc == own:
                    self_inc += amt
                    if self_inc in tg[inc]:
                        instr.then_inc(sems[inc], 1)
                else:
                    instr.then_inc(sems[inc], amt)


def build_program(dev_taps=False):
    nc = bass.Bass("TRN2", target_bir_lowering=False)

    uT = nc.dram_tensor("uT", [BLOC, D, L], F32, kind="ExternalInput")
    winT = nc.dram_tensor("winT", [D + 1, DP], F32, kind="ExternalInput")
    woutT = nc.dram_tensor("woutT", [DI + 1, D], F32, kind="ExternalInput")
    w1a_d = nc.dram_tensor("w1a", [D + 1, FF], F32, kind="ExternalInput")
    w2a_d = nc.dram_tensor("w2a", [FF + 1, D], F32, kind="ExternalInput")
    normw_d = nc.dram_tensor("normw", [D], F32, kind="ExternalInput")
    alog_d = nc.dram_tensor("alog", [H], F32, kind="ExternalInput")
    dd_d = nc.dram_tensor("dd", [H], F32, kind="ExternalInput")
    dtb_d = nc.dram_tensor("dtb", [H], F32, kind="ExternalInput")
    tri_d = nc.dram_tensor("tri", [128, 128], F32, kind="ExternalInput")
    idn_d = nc.dram_tensor("idn", [128, 128], F32, kind="ExternalInput")
    knt_d = nc.dram_tensor("knt", [128, NSEG], F32, kind="ExternalInput")
    ccol_d = nc.dram_tensor("ccol", [128, 4], F32, kind="ExternalInput")

    sk = "ExternalOutput" if dev_taps else "Internal"
    xnbf_s = nc.dram_tensor("xnbf_s", [BLOC, D, L], BF16, kind=sk)
    s_s = nc.dram_tensor("s_s", [BLOC, D, L], F32, kind=sk)
    outT = nc.dram_tensor("outT", [BLOC, D, L], F32, kind="ExternalOutput")

    def dslab(t, b, g, ktile=None, width=TG):
        if ktile is None:
            return bass.AP(t, b * D * L + g * TG,
                           [[L, 128], [128 * L, NKT], [1, width]])
        return bass.AP(t, b * D * L + ktile * 128 * L + g * TG,
                       [[L, 128], [1, width]])

    sch = Sched()
    stk = ExitStack()
    sems = {}
    with stk:
        for s in SEMS:
            sems[s] = stk.enter_context(nc.semaphore(s))

        def sb(name, shape, dt=F32):
            return stk.enter_context(nc.sbuf_tensor(name, shape, dt))

        tri = sb("tri_sb", [128, 128])
        idn = sb("idn_sb", [128, 128])
        knt = sb("knt_sb", [128, NSEG])
        ccol = sb("ccol_sb", [128, 4])
        normw = sb("normw_sb", [128, NKT])
        alog = sb("alog_sb", [H, 1])
        acol = sb("acol_sb", [H, 1])
        drow = sb("drow_sb", [1, H])
        dtb = sb("dtb_sb", [H, 1])
        dcol = sb("dcol_sb", [128, H])
        idh = sb("idh_sb", [128, 128 * H], BF16)
        ones_f = sb("ones_f", [128, 1])
        onesr_f = sb("onesr_f", [1, 128])
        onesr_bf = sb("onesr_bf", [1, TG], BF16)
        zrow = sb("zrow_sb", [1, Q])

        tc = []
        tc.append(sch.op("sync", "dma_start", out=tri[:], in_=tri_d.ap()[:, :],
                         inc="ld", amt=16))
        tc.append(sch.op("sync", "dma_start", out=idn[:], in_=idn_d.ap()[:, :],
                         inc="ld", amt=16))
        tc.append(sch.op("sync", "dma_start", out=knt[:], in_=knt_d.ap()[:, :],
                         inc="ld", amt=16))
        tc.append(sch.op("sync", "dma_start", out=ccol[:], in_=ccol_d.ap()[:, :],
                         inc="ld", amt=16))
        tc.append(sch.op("sync", "dma_start", out=normw[:],
                         in_=bass.AP(normw_d, 0, [[1, 128], [128, NKT]]),
                         allow_slow_non_contiguous=True,
                         inc="ld", amt=16))
        tc.append(sch.op("sync", "dma_start", out=alog[:],
                         in_=bass.AP(alog_d, 0, [[1, H], [1, 1]]),
                         inc="ld", amt=16))
        tc.append(sch.op("sync", "dma_start", out=drow[:],
                         in_=bass.AP(dd_d, 0, [[1, 1], [1, H]]),
                         inc="ld", amt=16))
        tc.append(sch.op("sync", "dma_start", out=dtb[:],
                         in_=bass.AP(dtb_d, 0, [[1, H], [1, 1]]),
                         inc="ld", amt=16))
        tc = tmax(*tc)

        t_ones = sch.op("vector", "memset", ones_f[:], 1.0, inc="vec")
        t_onesr = sch.op("vector", "memset", onesr_f[:], 1.0, inc="vec")
        t_onesbf = sch.op("vector", "memset", onesr_bf[:], 1.0, inc="vec")
        t_zrow = sch.op("vector", "memset", zrow[:], 0.0, inc="vec")
        t_acol = sch.op("scalar", "activation", out=acol[:], in_=alog[:],
                        func=AF.Exp, waits=tc, inc="act")
        t_acol = sch.op("vector", "tensor_single_scalar", out=acol[:],
                        in_=acol[:], scalar=-1.0, op=OP.mult,
                        waits=[t_acol], inc="vec")

        with ExitStack() as stkA:
            def sbA(n, s, dt=F32):
                return stkA.enter_context(nc.sbuf_tensor(n, s, dt))

            wA = [sbA(f"wA{k}", [128, DP], BF16) for k in range(NKT)]
            wAb = sbA("wAb", [1, DP], BF16)
            wO = [sbA(f"wO{k}", [128, D], BF16) for k in range(16)]
            wOb = sbA("wOb", [1, D], BF16)
            psA = [stkA.enter_context(nc.psum_tensor(f"psA{i}", [128, 512], F32))
                   for i in range(8)]
            (ps_proj, ps_xh, ps_y, ps_state, ps_g, ps_bc, ps_misc,
             ps_stats) = psA

            with nc.sbuf_tensor("wstageA", [128, DP], F32) as wstage:
                prev = None
                for k in range(NKT + 1):
                    nr = 128 if k < NKT else 1
                    dst = wA[k] if k < NKT else wAb
                    tl = sch.op("sync", "dma_start", out=wstage[0:nr, :],
                                in_=winT.ap()[k * 128:k * 128 + nr, :],
                                waits=[prev] if prev else [], inc="ld", amt=16)
                    prev = sch.op("vector", "tensor_copy", out=dst[0:nr, :],
                                  in_=wstage[0:nr, :], waits=[tl], inc="vec")
                for k in range(17):
                    nr = 128 if k < 16 else 1
                    dst = wO[k] if k < 16 else wOb
                    tl = sch.op("sync", "dma_start", out=wstage[0:nr, 0:D],
                                in_=woutT.ap()[k * 128:k * 128 + nr, :],
                                waits=[prev], inc="ld", amt=16)
                    prev = sch.op("vector", "tensor_copy", out=dst[0:nr, :],
                                  in_=wstage[0:nr, 0:D], waits=[tl], inc="vec")
                wA_ready = [prev]

            t_dc = sch.op("tensor", "matmul", ps_misc[:, 256:256 + H],
                          onesr_f[:, :], drow[:, :], start=True, stop=True,
                          waits=tmax(tc, [t_onesr]), inc="mm")
            t_dc = sch.op("scalar", "copy", out=dcol[:],
                          in_=ps_misc[:, 256:256 + H], waits=[t_dc], inc="act")
            sch.use("ps_misc", t_dc)
            t_idh = []
            for h in range(H):
                t_idh.append(sch.op(
                    "vector", "tensor_scalar",
                    out=idh[:, 128 * h:128 * (h + 1)], in0=idn[:],
                    scalar1=dcol[:, h:h + 1], scalar2=None, op0=OP.mult,
                    waits=tmax([t_dc], tc), inc="vec"))
            t_idh = tmax(*t_idh)

            u_g = sbA("u_g", [128, NKT * TG])
            xnc = [sbA(f"xnc{i}", [128, TG], BF16) for i in range(2)]
            un_bf = sbA("un_bf", [128, NKT * TG], BF16)
            usq = sbA("usq", [128, TG])
            rowst = sbA("rowst", [1, 512])
            rowm = sbA("rowm", [1, TG])
            rowr = sbA("rowr", [1, TG])
            rowb = sbA("rowb", [1, TG])
            abc = sbA("abc", [128, TG])
            bbc = sbA("bbc", [128, TG])
            z_silu = sbA("z_silu", [128, 16 * TG], BF16)
            xh_sb = [sbA(f"xh{c}", [128, DI], BF16) for c in range(NCH)]
            btok = [sbA(f"btok{c}", [128, 2 * DS * 2], BF16) for c in range(NCH)]
            bfeat = sbA("bfeat", [64, H * TG], BF16)
            ctil = sbA("ctil", [64, H * TG], BF16)
            dtw = sbA("dtw", [1, H * TG])
            stw = sbA("stw", [1, H * TG])
            cumw = sbA("cumw", [1, H * TG])
            erow = sbA("erow", [1, H * TG])
            dtp4 = sbA("dtp4", [H, TG])
            wrow = sbA("wrow", [1, H * TG])
            wcol = sbA("wcol", [128, H * NCH])
            lam = sbA("lam", [128, H * NCH])
            lamrow = sbA("lamrow", [1, H])
            ebc = sbA("ebc", [128, Q])
            w2t = sbA("w2t", [128, 128])
            m2 = [sbA(f"m2_{i}", [128, 128], BF16) for i in range(2)]
            wbbf = sbA("wbbf", [128, DS], BF16)
            hA = sbA("hA", [64, H * HD])
            hB = sbA("hB", [64, H * HD])
            hbf = sbA("hbf", [64, H * HD], BF16)
            s_sb = [sbA(f"s_sb{i}", [128, TG]) for i in range(2)]

            h_cur, h_nxt = hA, hB
            t_h = None
            t_hbf = None

            for b in range(BLOC):
                for g in range(NG):
                    t_lu = sch.op("sync", "dma_start", out=u_g[:],
                                  in_=dslab(uT, b, g),
                                  waits=tmax(sch.guard("u_g"), wA_ready),
                                  inc="ld", amt=16)

                    t_stU = t_stQ = None
                    for k in range(NKT):
                        ks = slice(k * TG, (k + 1) * TG)
                        tq = sch.op("scalar", "activation", out=usq[:],
                                    in_=u_g[:, ks], func=AF.Square,
                                    waits=tmax([t_lu], sch.guard("usq")),
                                    inc="act")
                        t_stU = sch.op("tensor", "matmul", ps_stats[0:1, 0:TG],
                                       ones_f[:, :], u_g[:, ks],
                                       start=(k == 0), stop=(k == NKT - 1),
                                       waits=tmax([t_lu], [t_ones],
                                                  sch.guard("ps_stats_u") if k == 0 else []),
                                       inc="mm")
                        sch.use("u_g", t_stU)
                        t_stQ = sch.op("tensor", "matmul",
                                       ps_state[0:1, 0:TG],
                                       ones_f[:, :], usq[:],
                                       start=(k == 0), stop=(k == NKT - 1),
                                       waits=tmax([tq],
                                                  sch.guard("ps_state") if k == 0 else []),
                                       inc="mm")
                        sch.use("usq", t_stQ)
                    t_r0 = sch.op("vector", "tensor_copy", out=rowst[:, 0:TG],
                                  in_=ps_stats[0:1, 0:TG],
                                  waits=[t_stU], inc="vec")
                    sch.use("ps_stats_u", t_r0)
                    t_r0b = sch.op("vector", "tensor_copy",
                                   out=rowst[:, 256:256 + TG],
                                   in_=ps_state[0:1, 0:TG],
                                   waits=[t_stQ], inc="vec")
                    sch.use("ps_state", t_r0b)
                    t_r0 = tmax([t_r0], [t_r0b])
                    t_m = sch.op("vector", "tensor_single_scalar", out=rowm[:],
                                 in_=rowst[:, 0:TG], scalar=1.0 / D, op=OP.mult,
                                 waits=t_r0, inc="vec")
                    t_v = sch.op("vector", "tensor_scalar", out=rowr[:],
                                 in0=rowst[:, 256:256 + TG], scalar1=1.0 / D,
                                 scalar2=LN_EPS, op0=OP.mult, op1=OP.add,
                                 waits=t_r0, inc="vec")
                    t_m2 = sch.op("vector", "tensor_tensor", out=rowb[:],
                                  in0=rowm[:], in1=rowm[:], op=OP.mult,
                                  waits=[t_m], inc="vec")
                    t_v = sch.op("vector", "tensor_tensor", out=rowr[:],
                                 in0=rowr[:], in1=rowb[:], op=OP.subtract,
                                 waits=[t_v, t_m2], inc="vec")
                    t_v = sch.op("scalar", "activation", out=rowr[:],
                                 in_=rowr[:], func=AF.Ln, waits=[t_v], inc="act")
                    t_v = sch.op("scalar", "activation", out=rowr[:],
                                 in_=rowr[:], func=AF.Exp, scale=-0.5,
                                 waits=[t_v], inc="act")
                    t_bb = sch.op("vector", "tensor_tensor", out=rowb[:],
                                  in0=rowm[:], in1=rowr[:], op=OP.mult,
                                  waits=[t_v, t_m], inc="vec")
                    t_bb = sch.op("vector", "tensor_single_scalar", out=rowb[:],
                                  in_=rowb[:], scalar=-1.0, op=OP.mult,
                                  waits=[t_bb], inc="vec")
                    t_abm = sch.op("tensor", "matmul", ps_bc[:, 0:TG],
                                   onesr_f[:, :], rowr[:, :], start=True,
                                   stop=True,
                                   waits=tmax([t_v], [t_onesr],
                                              sch.guard("ps_bc")),
                                   inc="mm")
                    t_bbm = sch.op("tensor", "matmul", ps_bc[:, 256:256 + TG],
                                   onesr_f[:, :], rowb[:, :], start=True,
                                   stop=True, waits=[t_bb], inc="mm")
                    t_ab = sch.op("scalar", "copy", out=abc[:],
                                  in_=ps_bc[:, 0:TG],
                                  waits=tmax([t_abm, t_bbm],
                                             sch.guard("abc")), inc="act")
                    t_bbm = sch.op("scalar", "copy", out=bbc[:],
                                   in_=ps_bc[:, 256:256 + TG],
                                   waits=tmax([t_bbm],
                                              sch.guard("bbc")), inc="act")
                    sch.use("ps_bc", t_ab)
                    sch.use("ps_bc", t_bbm)
                    t_xn = []
                    for k in range(NKT):
                        ks = slice(k * TG, (k + 1) * TG)
                        tt1 = sch.op("vector", "tensor_tensor", out=u_g[:, ks],
                                     in0=u_g[:, ks], in1=abc[:], op=OP.mult,
                                     waits=tmax([t_ab], [t_lu]), inc="vec")
                        tt1 = sch.op("vector", "tensor_tensor", out=u_g[:, ks],
                                     in0=u_g[:, ks], in1=bbc[:], op=OP.add,
                                     waits=[tt1, t_bbm], inc="vec")
                        sch.use("u_g", tt1)
                        sch.use("abc", tt1)
                        sch.use("bbc", tt1)
                        t_xn.append(tt1)
                    t_xn = tmax(*t_xn)
                    for k in range(NKT):
                        ks = slice(k * TG, (k + 1) * TG)
                        t_xc = sch.op("vector", "tensor_copy",
                                      out=xnc[k % 2][:], in_=u_g[:, ks],
                                      waits=tmax(t_xn,
                                                 sch.guard(f"xnc{k % 2}")),
                                      inc="vec")
                        sch.use("u_g", t_xc)
                        t_xs = sch.op("gpsimd", "dma_start",
                                      out=dslab(xnbf_s, b, g, ktile=k),
                                      in_=xnc[k % 2][:], waits=[t_xc],
                                      inc="st", amt=16)
                        sch.use(f"xnc{k % 2}", t_xs)

                    t_rs = None
                    for k in range(NKT):
                        ks = slice(k * TG, (k + 1) * TG)
                        tq = sch.op("scalar", "activation", out=usq[:],
                                    in_=u_g[:, ks], func=AF.Square,
                                    waits=tmax(t_xn, sch.guard("usq")),
                                    inc="act")
                        sch.use("u_g", tq)
                        t_rs = sch.op("tensor", "matmul", ps_stats[0:1, 0:TG],
                                      ones_f[:, :], usq[:], start=(k == 0),
                                      stop=(k == NKT - 1),
                                      waits=tmax([tq], t_r0,
                                                 sch.guard("ps_stats_u") if k == 0 else []),
                                      inc="mm")
                        sch.use("usq", t_rs)
                    t_r2 = sch.op("vector", "tensor_scalar", out=rowm[:],
                                  in0=ps_stats[0:1, 0:TG], scalar1=1.0 / D,
                                  scalar2=RMS_EPS, op0=OP.mult, op1=OP.add,
                                  waits=[t_rs], inc="vec")
                    sch.use("ps_stats_u", t_r2)
                    t_r2 = sch.op("scalar", "activation", out=rowm[:],
                                  in_=rowm[:], func=AF.Ln, waits=[t_r2],
                                  inc="act")
                    t_r2 = sch.op("scalar", "activation", out=rowm[:],
                                  in_=rowm[:], func=AF.Exp, scale=-0.5,
                                  waits=[t_r2], inc="act")
                    t_r2 = sch.op("tensor", "matmul", ps_bc[:, 0:TG],
                                  onesr_f[:, :], rowm[:, :], start=True,
                                  stop=True,
                                  waits=tmax([t_r2], sch.guard("ps_bc")),
                                  inc="mm")
                    t_r2 = sch.op("scalar", "copy", out=abc[:],
                                  in_=ps_bc[:, 0:TG],
                                  waits=tmax([t_r2], sch.guard("abc")),
                                  inc="act")
                    sch.use("ps_bc", t_r2)
                    gun = sch.guard("un_bf")
                    t_un = []
                    for k in range(NKT):
                        ks = slice(k * TG, (k + 1) * TG)
                        tu = sch.op("vector", "scalar_tensor_tensor",
                                    out=un_bf[:, ks], in0=u_g[:, ks],
                                    scalar=normw[:, k:k + 1], in1=abc[:],
                                    op0=OP.mult, op1=OP.mult,
                                    waits=tmax([t_r2], t_xn, gun), inc="vec")
                        sch.use("u_g", tu)
                        sch.use("abc", tu)
                        t_un.append(tu)
                    t_un = tmax(*t_un)

                    t_mm = None
                    for k in range(NKT):
                        t_mm = sch.op("tensor", "matmul", ps_proj[0:H, 0:TG],
                                      wA[k][:, DTOFF:DTOFF + H],
                                      un_bf[:, k * TG:(k + 1) * TG],
                                      start=(k == 0), stop=False,
                                      waits=tmax(t_un,
                                                 sch.guard("ps_proj") if k == 0 else []),
                                      inc="mm")
                    t_mm = sch.op("tensor", "matmul", ps_proj[0:H, 0:TG],
                                  wAb[:, DTOFF:DTOFF + H], onesr_bf[:, 0:TG],
                                  start=False, stop=True,
                                  waits=[t_mm, t_onesbf], inc="mm")
                    t_dt = sch.op("scalar", "activation", out=dtp4[:],
                                  in_=ps_proj[0:H, 0:TG], func=AF.Exp,
                                  bias=dtb[:, 0:1], scale=1.0,
                                  waits=tmax([t_mm], tc, sch.guard("dtp4")),
                                  inc="act")
                    sch.use("ps_proj", t_dt)
                    t_dt = sch.op("scalar", "add", out=dtp4[:], in_=dtp4[:],
                                  add=1.0, waits=[t_dt], inc="act")
                    t_dt = sch.op("scalar", "activation", out=dtp4[:],
                                  in_=dtp4[:], func=AF.Ln, waits=[t_dt],
                                  inc="act")
                    t_dtw = sch.op("gpsimd", "dma_start",
                                   out=bass.AP(dtw, 0,
                                               [[H * TG, 1], [TG, H], [1, TG]]),
                                   in_=dtp4[:],
                                   waits=tmax([t_dt], sch.guard("dtw")),
                                   inc="st", amt=16)
                    t_stp = sch.op("vector", "tensor_scalar", out=dtp4[:],
                                   in0=dtp4[:], scalar1=acol[:, 0:1],
                                   scalar2=None, op0=OP.mult,
                                   waits=tmax([t_dt], [t_acol], [t_dtw]),
                                   inc="vec")
                    sch.use("dtp4", t_stp)
                    t_stw = sch.op("gpsimd", "dma_start",
                                   out=bass.AP(stw, 0,
                                               [[H * TG, 1], [TG, H], [1, TG]]),
                                   in_=dtp4[:],
                                   waits=tmax([t_stp], sch.guard("stw")),
                                   inc="st", amt=16)
                    sch.use("dtp4", t_stw)
                    t_cum = []
                    gcum = sch.guard("cumw")
                    for h in range(H):
                        for c in range(NCH):
                            off = h * TG + c * Q
                            t_cum.append(sch.op(
                                "vector", "tensor_tensor_scan",
                                out=cumw[:, off:off + Q],
                                data0=stw[:, off:off + Q], data1=zrow[:, 0:Q],
                                initial=0.0, op0=OP.add, op1=OP.add,
                                waits=tmax([t_stw], [t_zrow], gcum),
                                inc="vec"))
                            sch.use("stw", t_cum[-1])
                    t_cum = tmax(*t_cum)
                    t_er = sch.op("scalar", "activation", out=erow[:],
                                  in_=cumw[:], func=AF.Exp,
                                  waits=tmax(t_cum, sch.guard("erow")),
                                  inc="act")
                    t_wr = sch.op("scalar", "activation", out=wrow[:],
                                  in_=cumw[:], func=AF.Exp, scale=-1.0,
                                  waits=tmax(t_cum, sch.guard("wrow")),
                                  inc="act")
                    sch.use("cumw", t_er)
                    sch.use("cumw", t_wr)
                    t_wr = sch.op("vector", "tensor_tensor", out=wrow[:],
                                  in0=wrow[:], in1=dtw[:], op=OP.mult,
                                  waits=[t_wr, t_dtw], inc="vec")
                    sch.use("dtw", t_wr)
                    t_wc = []
                    t_lm = []
                    gw = sch.guard("wcol")
                    gl = sch.guard("lam")
                    for c in range(NCH):
                        for h in range(H):
                            off = h * TG + c * Q
                            tt1 = sch.op("tensor", "transpose",
                                         ps_misc[0:Q, 128:129],
                                         wrow[:, off:off + Q], idn[0:1, 0:1],
                                         waits=tmax([t_wr],
                                                    sch.guard("ps_misc")),
                                         inc="mm")
                            tt2 = sch.op("vector", "tensor_copy",
                                         out=wcol[:, c * H + h:c * H + h + 1],
                                         in_=ps_misc[0:Q, 128:129],
                                         waits=tmax([tt1], gw), inc="vec")
                            sch.use("ps_misc", tt2)
                            sch.use("wrow", tt2)
                            t_wc.append(tt2)
                        tt3v = sch.op("vector", "tensor_copy",
                                      out=lamrow[0:1, 0:H],
                                      in_=bass.AP(erow, c * Q + Q - 1,
                                                  [[H * TG, 1], [TG, H]]),
                                      waits=tmax([t_er],
                                                 sch.guard("lamrow")),
                                      inc="vec")
                        sch.use("erow", tt3v)
                        tt3 = sch.op("tensor", "matmul",
                                     ps_misc[:, 136:136 + H], onesr_f[:, :],
                                     lamrow[0:1, 0:H],
                                     start=True, stop=True,
                                     waits=tmax([tt3v], [t_onesr],
                                                sch.guard("ps_misc")),
                                     inc="mm")
                        tt4 = sch.op("scalar", "copy",
                                     out=lam[:, c * H:(c + 1) * H],
                                     in_=ps_misc[:, 136:136 + H],
                                     waits=tmax([tt3], gl), inc="act")
                        sch.use("ps_misc", tt4)
                        sch.use("lamrow", tt4)
                        t_lm.append(tt4)
                    t_wc = tmax(*t_wc)
                    t_lm = tmax(*t_lm)

                    t_bf = []
                    gbf = sch.guard("bfeat")
                    for hh in range(H):
                        pb, pnm = ((ps_proj, "ps_proj") if hh % 2 == 0
                                   else (ps_bc, "ps_bc"))
                        col0 = BOFF + hh * 64
                        t_mm = None
                        for k in range(NKT):
                            t_mm = sch.op("tensor", "matmul",
                                          pb[0:64, 0:TG],
                                          wA[k][:, col0:col0 + 64],
                                          un_bf[:, k * TG:(k + 1) * TG],
                                          start=(k == 0), stop=False,
                                          waits=tmax(t_un,
                                                     sch.guard(pnm) if k == 0 else []),
                                          inc="mm")
                        t_mm = sch.op("tensor", "matmul", pb[0:64, 0:TG],
                                      wAb[:, col0:col0 + 64],
                                      onesr_bf[:, 0:TG], start=False,
                                      stop=True, waits=[t_mm], inc="mm")
                        te = sch.op("scalar", "copy",
                                    out=bfeat[:, hh * TG:(hh + 1) * TG],
                                    in_=pb[0:64, 0:TG],
                                    waits=tmax([t_mm], gbf), inc="act")
                        sch.use(pnm, te)
                        t_bf.append(te)
                    t_bf = tmax(*t_bf)

                    t_ct = []
                    gct = sch.guard("ctil")
                    for hh in range(H):
                        pb, pnm = ((ps_proj, "ps_proj") if hh % 2 == 0
                                   else (ps_bc, "ps_bc"))
                        col0 = COFF + hh * 64
                        t_mm = None
                        for k in range(NKT):
                            t_mm = sch.op("tensor", "matmul",
                                          pb[0:64, 0:TG],
                                          wA[k][:, col0:col0 + 64],
                                          un_bf[:, k * TG:(k + 1) * TG],
                                          start=(k == 0), stop=False,
                                          waits=tmax(t_un,
                                                     sch.guard(pnm) if k == 0 else []),
                                          inc="mm")
                        t_mm = sch.op("tensor", "matmul", pb[0:64, 0:TG],
                                      wAb[:, col0:col0 + 64],
                                      onesr_bf[:, 0:TG], start=False,
                                      stop=True, waits=[t_mm], inc="mm")
                        for c in range(NCH):
                            e0 = hh * TG + c * Q
                            tb1 = sch.op("tensor", "matmul", ps_misc[0:64, 0:Q],
                                         onesr_f[:, 0:64],
                                         erow[:, e0:e0 + Q], start=True,
                                         stop=True,
                                         waits=tmax([t_er],
                                                    sch.guard("ps_misc")),
                                         inc="mm")
                            tb3 = sch.op("scalar", "copy", out=ebc[0:64, :],
                                         in_=ps_misc[0:64, 0:Q],
                                         waits=tmax([tb1], sch.guard("ebc")),
                                         inc="act")
                            sch.use("ps_misc", tb3)
                            sch.use("erow", tb3)
                            tcv = sch.op("vector", "tensor_tensor",
                                         out=ctil[:, hh * TG + c * Q:hh * TG + c * Q + Q],
                                         in0=pb[0:64, c * Q:c * Q + Q],
                                         in1=ebc[0:64, :], op=OP.mult,
                                         waits=tmax([t_mm, tb3], gct),
                                         inc="vec")
                            sch.use("ebc", tcv)
                            sch.use(pnm, tcv)
                            t_ct.append(tcv)
                    t_ct = tmax(*t_ct)

                    t_zs = []
                    gz = sch.guard("z_silu")
                    for i in range(16):
                        pb, pnm = ((ps_proj, "ps_proj") if i % 2 == 0
                                   else (ps_bc, "ps_bc"))
                        col0 = ZOFF + i * 128
                        t_mm = None
                        for k in range(NKT):
                            t_mm = sch.op("tensor", "matmul",
                                          pb[:, 0:TG],
                                          wA[k][:, col0:col0 + 128],
                                          un_bf[:, k * TG:(k + 1) * TG],
                                          start=(k == 0), stop=False,
                                          waits=tmax(t_un,
                                                     sch.guard(pnm) if k == 0 else []),
                                          inc="mm")
                        t_mm = sch.op("tensor", "matmul",
                                      pb[:, 0:TG],
                                      wAb[:, col0:col0 + 128],
                                      onesr_bf[:, 0:TG], start=False,
                                      stop=True, waits=[t_mm], inc="mm")
                        te = sch.op("scalar", "activation",
                                    out=z_silu[:, i * TG:(i + 1) * TG],
                                    in_=pb[:, 0:TG],
                                    func=AF.Silu, waits=tmax([t_mm], gz),
                                    inc="act")
                        sch.use(pnm, te)
                        t_zs.append(te)

                    t_xh = {}
                    t_bt = {}
                    rr = 0
                    for c in range(NCH):
                        evs = []
                        for psl in range(4):
                            pb, pnm = ((ps_xh, "ps_xh") if rr % 2 == 0
                                       else (ps_y, "ps_y"))
                            rr += 1
                            t_mm = None
                            for k in range(NKT):
                                t_mm = sch.op(
                                    "tensor", "matmul", pb[:, 0:512],
                                    un_bf[:, k * TG + c * Q:k * TG + c * Q + Q],
                                    wA[k][:, XOFF + psl * 512:XOFF + (psl + 1) * 512],
                                    start=(k == 0), stop=False,
                                    waits=tmax(t_un,
                                               sch.guard(pnm) if k == 0 else []),
                                    inc="mm")
                            t_mm = sch.op("tensor", "matmul", pb[:, 0:512],
                                          onesr_bf[:, 0:Q],
                                          wAb[:, XOFF + psl * 512:XOFF + (psl + 1) * 512],
                                          start=False, stop=True,
                                          waits=[t_mm], inc="mm")
                            te = sch.op("scalar", "copy",
                                        out=xh_sb[c][:, psl * 512:(psl + 1) * 512],
                                        in_=pb[:, 0:512],
                                        waits=tmax([t_mm],
                                                   sch.guard(f"xh{c}_{psl}")),
                                        inc="act")
                            sch.use(pnm, te)
                            evs.append(te)
                        t_xh[c] = tmax(*evs)
                        pb, pnm = ((ps_xh, "ps_xh") if rr % 2 == 0
                                   else (ps_y, "ps_y"))
                        rr += 1
                        t_mm = None
                        for k in range(NKT):
                            t_mm = sch.op(
                                "tensor", "matmul", pb[:, 0:256],
                                un_bf[:, k * TG + c * Q:k * TG + c * Q + Q],
                                wA[k][:, BOFF:BOFF + 256],
                                start=(k == 0), stop=False,
                                waits=tmax(t_un,
                                           sch.guard(pnm) if k == 0 else []),
                                inc="mm")
                        t_mm = sch.op("tensor", "matmul", pb[:, 0:256],
                                      onesr_bf[:, 0:Q], wAb[:, BOFF:BOFF + 256],
                                      start=False, stop=True, waits=[t_mm],
                                      inc="mm")
                        t_bt[c] = sch.op("scalar", "copy", out=btok[c][:],
                                         in_=pb[:, 0:256],
                                         waits=tmax([t_mm],
                                                    sch.guard(f"btok{c}")),
                                         inc="act")
                        sch.use(pnm, t_bt[c])

                    sch.use("un_bf", t_bt[NCH - 1])
                    sch.use("un_bf", t_zs[-1])

                    for c in range(NCH):
                        first = (g == 0 and c == 0)
                        y_evacs = []
                        for h in range(H):
                            ccols = slice(h * TG + c * Q, h * TG + c * Q + Q)
                            t_w2 = sch.op("vector", "tensor_scalar",
                                          out=w2t[:], in0=tri[:],
                                          scalar1=wcol[:, c * H + h:c * H + h + 1],
                                          scalar2=None, op0=OP.mult,
                                          waits=tmax(t_wc, tc,
                                                     sch.guard("w2t")),
                                          inc="vec")
                            sch.use("wcol", t_w2)
                            t_gmm = sch.op("tensor", "matmul", ps_g[:, 0:Q],
                                           bfeat[0:64, ccols],
                                           ctil[0:64, ccols],
                                           start=True, stop=True,
                                           waits=tmax(t_bf, t_ct,
                                                      sch.guard("ps_g")),
                                           inc="mm")
                            t_m2 = sch.op("vector", "scalar_tensor_tensor",
                                          out=m2[h % 2][:], in0=ps_g[:, 0:Q],
                                          scalar=1.0, in1=w2t[:],
                                          op0=OP.bypass, op1=OP.mult,
                                          waits=tmax([t_gmm, t_w2],
                                                     sch.guard(f"m2_{h % 2}")),
                                          inc="vec")
                            sch.use("ps_g", t_m2)
                            sch.use("w2t", t_m2)
                            tprev = sch.guard("ps_y")
                            last_y = None
                            for psl in range(4):
                                pc = psl * Q
                                xsl = slice(h * HD + psl * Q, h * HD + (psl + 1) * Q)
                                tm1 = sch.op("tensor", "matmul",
                                             ps_y[:, pc:pc + Q],
                                             xh_sb[c][:, xsl], m2[h % 2][:],
                                             start=True, stop=False,
                                             waits=tmax([t_m2], [t_xh[c]],
                                                        tprev),
                                             inc="mm")
                                tm2 = sch.op("tensor", "matmul",
                                             ps_y[:, pc:pc + Q],
                                             xh_sb[c][:, xsl],
                                             idh[:, 128 * h:128 * (h + 1)],
                                             start=False, stop=first,
                                             waits=tmax([tm1], t_idh),
                                             inc="mm")
                                last_y = tm2
                                if not first:
                                    last_y = sch.op(
                                        "tensor", "matmul", ps_y[:, pc:pc + Q],
                                        hbf[0:64, xsl],
                                        ctil[0:64, ccols],
                                        start=False, stop=True,
                                        waits=tmax([tm2], [t_hbf]),
                                        inc="mm")
                                sch.use(f"m2_{h % 2}", last_y)
                                sch.use("hbf", last_y)
                            for psl in range(4):
                                zt = 4 * h + psl
                                gsl = slice(zt * TG + c * Q, zt * TG + c * Q + Q)
                                ge = sch.op("vector", "tensor_tensor",
                                            out=z_silu[:, gsl],
                                            in0=ps_y[:, psl * Q:(psl + 1) * Q],
                                            in1=z_silu[:, gsl], op=OP.mult,
                                            waits=tmax([last_y], [t_zs[zt]]),
                                            inc="vec")
                                sch.use("ps_y", ge)
                                sch.use("z_silu", ge)
                                y_evacs.append(ge)
                            t_wb = sch.op("vector", "tensor_scalar",
                                          out=wbbf[:],
                                          in0=btok[c][:, h * DS:(h + 1) * DS],
                                          scalar1=wcol[:, c * H + h:c * H + h + 1],
                                          scalar2=lam[:, c * H + h:c * H + h + 1],
                                          op0=OP.mult, op1=OP.mult,
                                          waits=tmax([t_bt[c]], t_wc, t_lm,
                                                     sch.guard("wbbf")),
                                          inc="vec")
                            sch.use("wcol", t_wb)
                            sch.use("lam", t_wb)
                            t_smm = sch.op("tensor", "matmul",
                                           ps_state[0:64, 0:512], wbbf[:],
                                           xh_sb[c][:, h * HD:(h + 1) * HD],
                                           start=True, stop=True,
                                           waits=tmax([t_wb], [t_xh[c]],
                                                      sch.guard("ps_state")),
                                           inc="mm")
                            sch.use("wbbf", t_smm)
                            hsl = slice(h * HD, (h + 1) * HD)
                            if first:
                                t_hn = sch.op("scalar", "copy",
                                              out=h_nxt[0:64, hsl],
                                              in_=ps_state[0:64, 0:512],
                                              waits=[t_smm], inc="act")
                            else:
                                t_hn = sch.op("vector", "scalar_tensor_tensor",
                                              out=h_nxt[0:64, hsl],
                                              in0=h_cur[0:64, hsl],
                                              scalar=lam[0:64, c * H + h:c * H + h + 1],
                                              in1=ps_state[0:64, 0:512],
                                              op0=OP.mult, op1=OP.add,
                                              waits=tmax([t_smm], [t_h] if t_h else [],
                                                         t_lm),
                                              inc="vec")
                                sch.use("lam", t_hn)
                            sch.use("ps_state", t_hn)
                            sch.use(f"xh{c}_{h}", t_hn)
                            sch.use("h_nxt", t_hn)
                        ghbf = sch.guard("hbf")
                        t_hbf = sch.op("vector", "tensor_copy", out=hbf[:],
                                       in_=h_nxt[:],
                                       waits=tmax(sch.guard("h_nxt"), ghbf),
                                       inc="vec")
                        t_h = t_hbf
                        h_cur, h_nxt = h_nxt, h_cur
                        for psl in range(4):
                            sch.use(f"xh{c}_{psl}", t_hbf)
                        sch.use(f"btok{c}", t_hbf)
                        t_y_all = tmax(*y_evacs)
                        if c == NCH - 1:
                            t_y_group = t_y_all
                    t_slast = None
                    for dti in range(NKT):
                        pb, pnm = ((ps_proj, "ps_proj") if dti % 2 == 0
                                   else (ps_bc, "ps_bc"))
                        t_mm = None
                        for pt in range(16):
                            t_mm = sch.op("tensor", "matmul", pb[:, 0:TG],
                                          wO[pt][:, dti * 128:(dti + 1) * 128],
                                          z_silu[:, pt * TG:(pt + 1) * TG],
                                          start=(pt == 0), stop=False,
                                          waits=tmax(t_y_group,
                                                     sch.guard(pnm) if pt == 0 else []),
                                          inc="mm")
                        t_mm = sch.op("tensor", "matmul", pb[:, 0:TG],
                                      wOb[:, dti * 128:(dti + 1) * 128],
                                      onesr_bf[:, 0:TG], start=False,
                                      stop=True, waits=[t_mm], inc="mm")
                        t_ev = sch.op("scalar", "copy",
                                      out=s_sb[dti % 2][:],
                                      in_=pb[:, 0:TG],
                                      waits=tmax([t_mm],
                                                 sch.guard(f"s_sb{dti % 2}")),
                                      inc="act")
                        sch.use(pnm, t_ev)
                        t_sst = sch.op("gpsimd", "dma_start",
                                       out=dslab(s_s, b, g, ktile=dti),
                                       in_=s_sb[dti % 2][:], waits=[t_ev],
                                       inc="st", amt=16)
                        sch.use(f"s_sb{dti % 2}", t_sst)
                        if dti == NKT - 1:
                            sch.use("z_silu", t_ev)

            phaseA_end = sch.snap()

        sch.barrier_all()
        with ExitStack() as stkB:
            def sbB(n, s, dt=F32):
                return stkB.enter_context(nc.sbuf_tensor(n, s, dt))

            w1 = [sbB(f"w1_{k}", [128, FF], BF16) for k in range(NKT)]
            w1b = sbB("w1b", [1, FF], BF16)
            w2 = [sbB(f"w2_{k}", [128, D], BF16) for k in range(32)]
            w2b = sbB("w2b", [1, D], BF16)
            ps_q = stkB.enter_context(nc.psum_tensor("ps_q", [128, 1024], F32))
            ps_o = stkB.enter_context(nc.psum_tensor("ps_o", [128, 512], F32))
            ps_h = stkB.enter_context(nc.psum_tensor("ps_h", [128, 1024], F32))

            with nc.sbuf_tensor("wstageB", [128, FF], F32) as wstage:
                prev = None
                for k in range(NKT + 1):
                    nr = 128 if k < NKT else 1
                    dst = w1[k] if k < NKT else w1b
                    tl = sch.op("sync", "dma_start", out=wstage[0:nr, :],
                                in_=w1a_d.ap()[k * 128:k * 128 + nr, :],
                                waits=[prev] if prev else [], inc="ld", amt=16)
                    prev = sch.op("vector", "tensor_copy", out=dst[0:nr, :],
                                  in_=wstage[0:nr, :], waits=[tl], inc="vec")
                for k in range(33):
                    nr = 128 if k < 32 else 1
                    tl = sch.op("sync", "dma_start", out=wstage[0:nr, 0:D],
                                in_=w2a_d.ap()[k * 128:k * 128 + nr, :],
                                waits=[prev], inc="ld", amt=16)
                    if k < 32:
                        prev = sch.op("vector", "tensor_scalar",
                                      out=w2[k][:], in0=wstage[0:128, 0:D],
                                      scalar1=1.0 / 256.0, scalar2=None,
                                      op0=OP.mult, waits=[tl], inc="vec")
                    else:
                        prev = sch.op("vector", "tensor_copy", out=w2b[0:1, :],
                                      in_=wstage[0:1, 0:D], waits=[tl],
                                      inc="vec")
                wB_ready = [prev]

            xnbf_g = sbB("xnbf_g", [128, NKT * TG], BF16)
            uc = sbB("uc", [128, 1024])
            casA = sbB("casA", [128, 1024])
            casB = sbB("casB", [128, 1024])
            st1 = sbB("st1", [128, 1024])
            tmpr = [sbB(f"tmpr{i}", [128, 1024]) for i in range(3)]
            accA = sbB("accA", [128, 1024])
            accB = sbB("accB", [128, 1024])
            hq_sb = sbB("hq_sb", [128, 32 * TG], BF16)
            y3 = sbB("y3", [128, 2 * TG])
            y3c, c3A, c3B, st3 = accA, casA, casB, st1
            EP = slice(0, 2 * TG)
            gq = sbB("gq", [128, 2 * TG])
            s_t = sbB("s_t", [128, 2 * TG])
            u_t = sbB("u_t", [128, 2 * TG])

            for b in range(BLOC):
                for g in range(NG):
                    t_lx = sch.op("sync", "dma_start", out=xnbf_g[:],
                                  in_=dslab(xnbf_s, b, g),
                                  waits=tmax(sch.guard("xnbf_g"), wB_ready),
                                  inc="ld", amt=16)
                    for qd in range(8):
                        gq_ps = sch.guard("ps_q")
                        t_mm = None
                        for j in range(4):
                            ft = 4 * qd + j
                            jc = slice(j * TG, (j + 1) * TG)
                            for k in range(NKT):
                                t_mm = sch.op(
                                    "tensor", "matmul", ps_q[:, jc],
                                    w1[k][:, ft * 128:(ft + 1) * 128],
                                    xnbf_g[:, k * TG:(k + 1) * TG],
                                    start=(k == 0), stop=False,
                                    waits=tmax([t_lx],
                                               gq_ps if (k == 0 and j == 0) else []),
                                    inc="mm")
                            t_mm = sch.op("tensor", "matmul", ps_q[:, jc],
                                          w1b[:, ft * 128:(ft + 1) * 128],
                                          onesr_bf[:, 0:TG], start=False,
                                          stop=True, waits=[t_mm, t_onesbf],
                                          inc="mm")
                        t_y5 = sch.op("scalar", "activation", out=uc[:],
                                      in_=ps_q[:, 0:1024], func=AF.Copy,
                                      bias=2048.5, scale=256.0,
                                      waits=tmax([t_mm], sch.guard("uc")),
                                      inc="act")
                        sch.use("ps_q", t_y5)
                        LE = ("vector", "vector")
                        CS = [slice(0, 1024)]
                        NLANE = 1
                        t_cl = {}
                        for ln in range(NLANE):
                            t_cl[ln] = sch.op(LE[ln], "tensor_scalar",
                                              out=uc[:, CS[ln]],
                                              in0=uc[:, CS[ln]], scalar1=0.5,
                                              scalar2=4095.5, op0=OP.max,
                                              op1=OP.min,
                                              waits=tmax([t_y5]),
                                              inc="vec")
                        t_yk = {}
                        for ln in range(NLANE):
                            cs = CS[ln]
                            sem = "vec"
                            src, dst = uc, casA
                            tk = t_cl[ln]
                            for bi in range(11, -1, -1):
                                pw = float(2 ** bi)
                                ts_ = sch.op(LE[ln], "tensor_scalar",
                                             out=st1[:, cs], in0=src[:, cs],
                                             scalar1=pw, scalar2=pw,
                                             op0=OP.is_ge, op1=OP.mult,
                                             waits=[tk], inc=sem)
                                tk = sch.op(LE[ln], "tensor_tensor",
                                            out=dst[:, cs], in0=src[:, cs],
                                            in1=st1[:, cs], op=OP.subtract,
                                            waits=[ts_], inc=sem)
                                src, dst = (casA, casB) if src is uc else \
                                           ((casB, casA) if src is casA else (casA, casB))
                            t_yk[ln] = sch.op(LE[ln], "tensor_tensor",
                                              out=dst[:, cs], in0=uc[:, cs],
                                              in1=src[:, cs], op=OP.subtract,
                                              waits=[tk], inc=sem)
                            yk_buf = dst
                        for ln in range(NLANE):
                            sch.use("uc", t_yk[ln])
                        t_relu = {}
                        t_mac = {0: None}
                        asrc, adst = accA, accB
                        gacc = tmax(sch.guard("accA"), sch.guard("accB"))
                        for j in range(NSEG):
                            wl = [t_yk[ln2] for ln2 in range(NLANE)]
                            if j >= 3:
                                wl.append(t_relu[j - 3][1])
                            t_r = sch.op("scalar", "activation",
                                         out=tmpr[j % 3][:], in_=yk_buf[:],
                                         func=AF.Relu, bias=knt[:, j:j + 1],
                                         scale=1.0, waits=tmax(*wl, tc),
                                         inc="act")
                            if j == 0:
                                tm = sch.op("vector", "tensor_scalar",
                                            out=adst[:], in0=tmpr[0][:],
                                            scalar1=GAM[0], scalar2=BASE2,
                                            op0=OP.mult, op1=OP.add,
                                            waits=tmax([t_r], gacc), inc="vec")
                            elif j == 1:
                                tm = sch.op("vector", "tensor_scalar",
                                            out=st1[:], in0=tmpr[1][:],
                                            scalar1=GAM[1], scalar2=Y0,
                                            op0=OP.mult, op1=OP.add,
                                            waits=tmax([t_r], gacc), inc="vec")
                                tm = sch.op("vector", "tensor_tensor",
                                            out=adst[:], in0=asrc[:],
                                            in1=st1[:], op=OP.add,
                                            waits=[tm, t_mac[0]], inc="vec")
                            else:
                                tm = sch.op("vector", "scalar_tensor_tensor",
                                            out=adst[:], in0=tmpr[j % 3][:],
                                            scalar=GAM[j], in1=asrc[:],
                                            op0=OP.mult, op1=OP.add,
                                            waits=[t_r, t_mac[0]], inc="vec")
                            t_mac[0] = tm
                            t_relu[j] = (t_r, tm)
                            asrc, adst = adst, asrc
                        t_hmm = t_mac[0]
                        oth = casA if yk_buf is casB else casB
                        t_hqf = {}
                        for ln in range(NLANE):
                            cs = CS[ln]
                            sem = "vec"
                            tk = t_hmm
                            src, dst = asrc, oth
                            for bi in range(11, -1, -1):
                                pw = float(2 ** bi)
                                ts_ = sch.op(LE[ln], "tensor_scalar",
                                             out=st1[:, cs], in0=src[:, cs],
                                             scalar1=pw, scalar2=pw,
                                             op0=OP.is_ge, op1=OP.mult,
                                             waits=[tk], inc=sem)
                                tk = sch.op(LE[ln], "tensor_tensor",
                                            out=dst[:, cs], in0=src[:, cs],
                                            in1=st1[:, cs], op=OP.subtract,
                                            waits=[ts_], inc=sem)
                                if src is asrc:
                                    src, dst = oth, yk_buf
                                else:
                                    src, dst = dst, src
                            t_hqi = sch.op(LE[ln], "tensor_tensor",
                                           out=st1[:, cs], in0=asrc[:, cs],
                                           in1=src[:, cs], op=OP.subtract,
                                           waits=[tk], inc=sem)
                            qcol = slice(qd * 1024 + cs.start,
                                         qd * 1024 + cs.stop)
                            t_hqf[ln] = sch.op(LE[ln], "tensor_scalar",
                                               out=hq_sb[:, qcol],
                                               in0=st1[:, cs], scalar1=-1024.0,
                                               scalar2=None, op0=OP.add,
                                               waits=tmax([t_hqi],
                                                          sch.guard(f"hq{qd}_{ln}")),
                                               inc=sem)
                        for ln in range(NLANE):
                            sch.use("accA", t_hqf[ln])
                            sch.use("accB", t_hqf[ln])
                            sch.use(f"hqtok{qd}", t_hqf[ln])
                    sch.use("xnbf_g", t_mm)
                    t_hq_all = [sch.guard(f"hqtok{qd}") for qd in range(8)]
                    t_hq_all = tmax(*t_hq_all)

                    for dp in range(NKT // 2):
                        t_mm = None
                        for half in range(2):
                            dti = 2 * dp + half
                            pcol = half * TG
                            for ft in range(32):
                                t_mm = sch.op(
                                    "tensor", "matmul",
                                    ps_o[:, pcol:pcol + TG],
                                    w2[ft][:, dti * 128:(dti + 1) * 128],
                                    hq_sb[:, ft * TG:(ft + 1) * TG],
                                    start=(ft == 0), stop=False,
                                    waits=tmax(t_hq_all,
                                               sch.guard("ps_o")
                                               if (ft == 0 and half == 0)
                                               else []),
                                    inc="mm")
                            t_mm = sch.op("tensor", "matmul",
                                          ps_o[:, pcol:pcol + TG],
                                          w2b[:, dti * 128:(dti + 1) * 128],
                                          onesr_bf[:, 0:TG], start=False,
                                          stop=True, waits=[t_mm], inc="mm")
                        for qd in range(8):
                            sch.use(f"hq{qd}_0", t_mm)
                            sch.use(f"hq{qd}_1", t_mm)
                        t_y3 = sch.op("scalar", "activation", out=y3[:],
                                      in_=ps_o[:, 0:2 * TG],
                                      func=AF.Copy, bias=512.5, scale=256.0,
                                      waits=tmax([t_mm], sch.guard("y3")),
                                      inc="act")
                        sch.use("ps_o", t_y3)
                        lne = "vector"
                        lsem = "vec"
                        t_c = sch.op(lne, "tensor_scalar", out=y3c[:, EP],
                                     in0=y3[:], scalar1=0.5, scalar2=1023.5,
                                     op0=OP.max, op1=OP.min,
                                     waits=tmax([t_y3]),
                                     inc=lsem)
                        sch.use("y3", t_c)
                        src, dst = y3c, c3A
                        tk = t_c
                        for bi in range(9, -1, -1):
                            pw = float(2 ** bi)
                            ts_ = sch.op(lne, "tensor_scalar", out=st3[:, EP],
                                         in0=src[:, EP], scalar1=pw,
                                         scalar2=pw,
                                         op0=OP.is_ge, op1=OP.mult,
                                         waits=[tk], inc=lsem)
                            tk = sch.op(lne, "tensor_tensor",
                                        out=dst[:, EP], in0=src[:, EP],
                                        in1=st3[:, EP],
                                        op=OP.subtract,
                                        waits=[ts_], inc=lsem)
                            src, dst = (c3A, c3B) if src is y3c else \
                                       ((c3B, c3A) if src is c3A else (c3A, c3B))
                        t_q2 = sch.op(lne, "tensor_tensor",
                                      out=dst[:, EP], in0=y3c[:, EP],
                                      in1=src[:, EP],
                                      op=OP.subtract,
                                      waits=[tk], inc=lsem)
                        fl_buf = dst
                        t_g = sch.op("scalar", "activation", out=gq[:],
                                     in_=fl_buf[:, EP], func=AF.Silu,
                                     bias=ccol[:, 0:1], scale=1.0 / 256.0,
                                     waits=tmax([t_q2], tc, sch.guard("gq")),
                                     inc="act")
                        t_ls = sch.guard("s_t")
                        t_lu2 = sch.guard("u_t")
                        for half in range(2):
                            hc = slice(half * TG, (half + 1) * TG)
                            t_ls = tmax([sch.op(
                                "sync", "dma_start", out=s_t[:, hc],
                                in_=dslab(s_s, b, g, ktile=2 * dp + half),
                                waits=t_ls, inc="ld", amt=16)])
                            t_lu2 = tmax([sch.op(
                                "sync", "dma_start", out=u_t[:, hc],
                                in_=dslab(uT, b, g, ktile=2 * dp + half),
                                waits=t_lu2, inc="ld", amt=16)])
                        t_o = sch.op("vector", "tensor_tensor", out=s_t[:],
                                     in0=s_t[:], in1=gq[:], op=OP.mult,
                                     waits=tmax([t_g], t_ls), inc="vec")
                        sch.use("gq", t_o)
                        t_o = sch.op("vector", "tensor_tensor", out=s_t[:],
                                     in0=s_t[:], in1=u_t[:], op=OP.add,
                                     waits=tmax([t_o], t_lu2), inc="vec")
                        sch.use("u_t", t_o)
                        for half in range(2):
                            hc = slice(half * TG, (half + 1) * TG)
                            t_st = sch.op("gpsimd", "dma_start",
                                          out=dslab(outT, b, g,
                                                    ktile=2 * dp + half),
                                          in_=s_t[:, hc], waits=[t_o],
                                          inc="st", amt=16)
                            sch.use("s_t", t_st)

            sch.wait_only("gpsimd", [("st", sch.count["st"])])

        with nc.Block() as block:
            @block.sync
            def _(eng):
                sch.replay("sync", eng, sems)

            @block.gpsimd
            def _(eng):
                sch.replay("gpsimd", eng, sems)

            @block.tensor
            def _(eng):
                sch.replay("tensor", eng, sems)

            @block.vector
            def _(eng):
                sch.replay("vector", eng, sems)

            @block.scalar
            def _(eng):
                sch.replay("scalar", eng, sems)

    return nc


_NC_CACHE = {}


def kernel(**inputs):
    u = np.asarray(inputs["L0_attn_u"], np.float32)
    Win = np.asarray(inputs["L0_attn_in_proj_w"], np.float32)
    bin_ = np.asarray(inputs["L0_attn_in_proj_b"], np.float32)
    normw = np.asarray(inputs["L0_attn_norm_w"], np.float32)
    alog = np.asarray(inputs["L0_attn_A_log"], np.float32)
    dd = np.asarray(inputs["L0_attn_D"], np.float32)
    dtb = np.asarray(inputs["L0_attn_dt_bias"], np.float32)
    Wout = np.asarray(inputs["L0_attn_out_proj_w"], np.float32)
    bout = np.asarray(inputs["L0_attn_out_proj_b"], np.float32)
    w1 = np.asarray(inputs["L0_ffn_w1"], np.float32)
    b1 = np.asarray(inputs["L0_ffn_b1"], np.float32)
    w2 = np.asarray(inputs["L0_ffn_w2"], np.float32)
    b2 = np.asarray(inputs["L0_ffn_b2"], np.float32)

    perm = np.concatenate(
        [np.arange(4100)]
        + [4100 + 128 * h + np.arange(64) for h in range(4)]
        + [4100 + 128 * h + 64 + np.arange(64) for h in range(4)])
    winT = np.ascontiguousarray(
        np.concatenate([Win[perm].T, bin_[perm][None, :]], 0), np.float32)
    woutT = np.ascontiguousarray(
        np.concatenate([Wout.T, bout[None, :]], 0), np.float32)
    w1a = np.ascontiguousarray(np.concatenate([w1, b1[None, :]], 0), np.float32)
    w2a = np.ascontiguousarray(np.concatenate([w2, b2[None, :]], 0), np.float32)
    tri = np.triu(np.ones((128, 128), np.float32))
    idn = np.eye(128, dtype=np.float32)
    knots = -(64.0 * np.arange(NSEG) + 1024.0)
    knt = np.ascontiguousarray(
        np.broadcast_to(knots[None, :], (128, NSEG)), np.float32)
    ccol = np.zeros((128, 4), np.float32)
    ccol[:, 0] = -2.0

    uT = np.ascontiguousarray(
        u.reshape(NCORE, BLOC, L, D).transpose(0, 1, 3, 2), np.float32)

    if "nc" not in _NC_CACHE:
        _NC_CACHE["nc"] = build_program()
    nc = _NC_CACHE["nc"]

    shared = dict(winT=winT, woutT=woutT, w1a=w1a, w2a=w2a, normw=normw,
                  alog=alog, dd=dd, dtb=dtb, tri=tri, idn=idn, knt=knt,
                  ccol=ccol)
    in_maps = [dict(uT=uT[i], **shared) for i in range(NCORE)]
    res = run_bass_kernel_spmd(nc, in_maps, core_ids=list(range(NCORE)))
    outs = [np.asarray(res.results[i]["outT"]) for i in range(NCORE)]
    out = np.concatenate(outs, 0).reshape(BFULL, D, L).transpose(0, 2, 1)
    return np.ascontiguousarray(out, np.float32)


if __name__ == "__main__":
    nc = build_program()
    n = sum(1 for _ in ())
    print("build ok")


# version 20
# speedup vs baseline: 1.0274x; 1.0274x over previous
import numpy as np
import ml_dtypes
import concourse.bass as bass
import concourse.mybir as mybir
from concourse.bass_utils import run_bass_kernel_spmd
from contextlib import ExitStack

F32 = mybir.dt.float32
BF16 = mybir.dt.bfloat16
OP = mybir.AluOpType
AF = mybir.ActivationFunctionType

NCORE = 8
BFULL, L, D = 16, 2048, 1024
BLOC = BFULL // NCORE
DI, H, DS, HD = 2048, 4, 64, 512
DP, FF = 4612, 4096
TG = 256
SPQ = 512
SPY = 160
NG = L // TG
Q = 128
NCH = TG // Q
NKT = D // 128
RMS_EPS = 1.1920929e-07
LN_EPS = 1e-5
ZOFF, XOFF, DTOFF, BOFF, COFF = 0, 2048, 4096, 4100, 4356

_PWL_Y_BITS = [
    0xb904c000, 0xb9adbb00, 0xba556b00, 0xbaf5ce80, 0xbb84b340, 0xbc064180,
    0xbc7e5920, 0xbce15244, 0xbd3a5e80, 0xbd8f926e, 0xbdcd3b48, 0xbe073b54,
    0xbe227686, 0xbe2e0cbe, 0xbe1df8a2, 0xbdcd765d, 0x00000000, 0x3e1944d2,
    0x3eb103af, 0x3f147cd0, 0x3f57625e, 0x3f8f1895, 0x3fb32c4c, 0x3fd706d9,
    0x3ffa2d0c, 0x400e3d5c, 0x401f01a7, 0x402f79be, 0x403fbda6, 0x404fe146,
    0x405ff2aa, 0x406ffa93, 0x407ffded,
]
PWL_Y = np.array(_PWL_Y_BITS, dtype=np.uint32).view(np.float32)
YP64 = PWL_Y.astype(np.float64) * 256.0
_SL = [(YP64[j + 1] - YP64[j]) / 64.0 for j in range(32)]
GAM = [float(np.float32(_SL[0]))] + \
      [float(np.float32(_SL[j] - _SL[j - 1])) for j in range(1, 32)] + \
      [float(np.float32(1.0 - _SL[31]))]
NSEG = 33
Y0 = float(np.float32(YP64[0]))
BASE2 = 1024.5

ENGS = ("sync", "gpsimd", "tensor", "vector", "scalar")
SEMS = ("ld", "ld2", "ld3", "st", "st2", "sb", "mm", "vec", "act")


def tmax(*tokens):
    best = {}

    def _add(t):
        if not t:
            return
        if isinstance(t, tuple) and len(t) == 2 and isinstance(t[0], str):
            s, v = t
            if v > best.get(s, 0):
                best[s] = v
            return
        for x in t:
            _add(x)

    for t in tokens:
        _add(t)
    return [(s, v) for s, v in best.items()]


class Sched:

    def __init__(self):
        self.items = {e: [] for e in ENGS}
        self.count = {s: 0 for s in SEMS}
        self._barrier = {e: None for e in ENGS}
        self._cons = {}

    def snap(self):
        return dict(self.count)

    def barrier_all(self):
        snap = [(s, self.count[s]) for s in SEMS if self.count[s] > 0]
        for e in ENGS:
            self._barrier[e] = snap

    SEM_ALIAS = {}

    def op(self, eng, method, *args, waits=(), inc=None, amt=1, **kwargs):
        inc = self.SEM_ALIAS.get(inc, inc)
        w = tmax(list(waits) if not isinstance(waits, list) else waits)
        if method == "dma_start" and inc not in ("mm", "vec", "act"):
            w = tmax(w, [(inc, self.count[inc])])
        if self._barrier[eng]:
            w = tmax(w, self._barrier[eng])
            self._barrier[eng] = None
        self.items[eng].append((tuple(w), method, args, kwargs, inc, amt))
        if inc is not None:
            self.count[inc] += amt
            return (inc, self.count[inc])
        return None

    def wait_only(self, eng, waits):
        self.items[eng].append((tuple(tmax(list(waits))), None, (), {}, None, 0))

    def use(self, name, tok):
        if tok:
            self._cons.setdefault(name, []).append(tok)

    def guard(self, name):
        toks = self._cons.pop(name, [])
        return tmax(*toks)

    OWN = {"tensor": "mm", "vector": "vec", "scalar": "act"}

    def _targets(self):
        if getattr(self, "_tg", None) is not None:
            return self._tg
        tg = {s: set() for s in SEMS}
        for eng_name in ENGS:
            own = self.OWN.get(eng_name)
            self_inc = 0
            hw = {}
            for waits, method, args, kwargs, inc, amt in self.items[eng_name]:
                for (s, v) in waits:
                    if s == own and v <= self_inc:
                        continue
                    if v > hw.get(s, 0):
                        tg[s].add(v)
                        hw[s] = v
                if inc == own:
                    self_inc += amt
        self._rank = {}
        for s in ("mm", "vec", "act"):
            self._rank[s] = {v: i + 1 for i, v in enumerate(sorted(tg[s]))}
        self._tg = tg
        return tg

    def replay(self, eng_name, eng, sems):
        tg = self._targets()
        own = self.OWN.get(eng_name)
        self_inc = 0
        hw = {}
        for waits, method, args, kwargs, inc, amt in self.items[eng_name]:
            for (s, v) in waits:
                if s == own and v <= self_inc:
                    continue
                if v > hw.get(s, 0):
                    eng.wait_ge(sems[s], self._rank[s][v] if s in self._rank
                                else v)
                    hw[s] = v
            if method is None:
                if inc == own:
                    self_inc += amt
                continue
            try:
                instr = getattr(eng, method)(*args, **kwargs)
            except Exception:
                import sys
                print(f"REPLAY FAIL {eng_name} {method} args={args} kwargs={list(kwargs.items())[:4]}",
                      file=sys.stderr)
                raise
            if inc is not None:
                if inc == own:
                    self_inc += amt
                    if self_inc in tg[inc]:
                        instr.then_inc(sems[inc], 1)
                else:
                    instr.then_inc(sems[inc], amt)


def build_program(dev_taps=False):
    nc = bass.Bass("TRN2", target_bir_lowering=False)

    uT = nc.dram_tensor("uT", [BLOC, D, L], F32, kind="ExternalInput")
    winT = nc.dram_tensor("winT", [D + 1, DP], F32, kind="ExternalInput")
    woutT = nc.dram_tensor("woutT", [DI + 1, D], F32, kind="ExternalInput")
    w1a_d = nc.dram_tensor("w1a", [D + 1, FF], F32, kind="ExternalInput")
    w2a_d = nc.dram_tensor("w2a", [FF + 1, D], F32, kind="ExternalInput")
    normw_d = nc.dram_tensor("normw", [D], F32, kind="ExternalInput")
    alog_d = nc.dram_tensor("alog", [H], F32, kind="ExternalInput")
    dd_d = nc.dram_tensor("dd", [H], F32, kind="ExternalInput")
    dtb_d = nc.dram_tensor("dtb", [H], F32, kind="ExternalInput")
    tri_d = nc.dram_tensor("tri", [128, 128], F32, kind="ExternalInput")
    idn_d = nc.dram_tensor("idn", [128, 128], F32, kind="ExternalInput")
    knt_d = nc.dram_tensor("knt", [128, NSEG], F32, kind="ExternalInput")
    ccol_d = nc.dram_tensor("ccol", [128, 4], F32, kind="ExternalInput")

    sk = "ExternalOutput" if dev_taps else "Internal"
    xnbf_s = nc.dram_tensor("xnbf_s", [BLOC, D, L], BF16, kind=sk)
    s_s = nc.dram_tensor("s_s", [BLOC, D, L], F32, kind=sk)
    outT = nc.dram_tensor("outT", [BLOC, D, L], F32, kind="ExternalOutput")

    def dslab(t, b, g, ktile=None, width=TG):
        if ktile is None:
            return bass.AP(t, b * D * L + g * TG,
                           [[L, 128], [128 * L, NKT], [1, width]])
        return bass.AP(t, b * D * L + ktile * 128 * L + g * TG,
                       [[L, 128], [1, width]])

    sch = Sched()
    stk = ExitStack()
    sems = {}
    with stk:
        for s in SEMS:
            sems[s] = stk.enter_context(nc.semaphore(s))

        def sb(name, shape, dt=F32):
            return stk.enter_context(nc.sbuf_tensor(name, shape, dt))

        tri = sb("tri_sb", [128, 128])
        idn = sb("idn_sb", [128, 128])
        knt = sb("knt_sb", [128, NSEG])
        ccol = sb("ccol_sb", [128, 4])
        normw = sb("normw_sb", [128, NKT])
        alog = sb("alog_sb", [H, 1])
        acol = sb("acol_sb", [H, 1])
        drow = sb("drow_sb", [1, H])
        dtb = sb("dtb_sb", [H, 1])
        dcol = sb("dcol_sb", [128, H])
        idh = sb("idh_sb", [128, 128 * H], BF16)
        ones_f = sb("ones_f", [128, 1])
        onesr_f = sb("onesr_f", [1, 128])
        onesr_bf = sb("onesr_bf", [1, TG], BF16)
        zrow = sb("zrow_sb", [1, Q])

        tc = []
        tc.append(sch.op("sync", "dma_start", out=tri[:], in_=tri_d.ap()[:, :],
                         inc="ld", amt=16))
        tc.append(sch.op("sync", "dma_start", out=idn[:], in_=idn_d.ap()[:, :],
                         inc="ld", amt=16))
        tc.append(sch.op("sync", "dma_start", out=knt[:], in_=knt_d.ap()[:, :],
                         inc="ld", amt=16))
        tc.append(sch.op("sync", "dma_start", out=ccol[:], in_=ccol_d.ap()[:, :],
                         inc="ld", amt=16))
        tc.append(sch.op("sync", "dma_start", out=normw[:],
                         in_=bass.AP(normw_d, 0, [[1, 128], [128, NKT]]),
                         allow_slow_non_contiguous=True,
                         inc="ld", amt=16))
        tc.append(sch.op("sync", "dma_start", out=alog[:],
                         in_=bass.AP(alog_d, 0, [[1, H], [1, 1]]),
                         inc="ld", amt=16))
        tc.append(sch.op("sync", "dma_start", out=drow[:],
                         in_=bass.AP(dd_d, 0, [[1, 1], [1, H]]),
                         inc="ld", amt=16))
        tc.append(sch.op("sync", "dma_start", out=dtb[:],
                         in_=bass.AP(dtb_d, 0, [[1, H], [1, 1]]),
                         inc="ld", amt=16))
        tc = tmax(*tc)

        t_ones = sch.op("vector", "memset", ones_f[:], 1.0, inc="vec")
        t_onesr = sch.op("vector", "memset", onesr_f[:], 1.0, inc="vec")
        t_onesbf = sch.op("vector", "memset", onesr_bf[:], 1.0, inc="vec")
        t_zrow = sch.op("vector", "memset", zrow[:], 0.0, inc="vec")
        t_acol = sch.op("scalar", "activation", out=acol[:], in_=alog[:],
                        func=AF.Exp, waits=tc, inc="act")
        t_acol = sch.op("vector", "tensor_single_scalar", out=acol[:],
                        in_=acol[:], scalar=-1.0, op=OP.mult,
                        waits=[t_acol], inc="vec")

        with ExitStack() as stkA:
            def sbA(n, s, dt=F32):
                return stkA.enter_context(nc.sbuf_tensor(n, s, dt))

            wA = [sbA(f"wA{k}", [128, DP], BF16) for k in range(NKT)]
            wAb = sbA("wAb", [1, DP], BF16)
            wO = [sbA(f"wO{k}", [128, D], BF16) for k in range(16)]
            wOb = sbA("wOb", [1, D], BF16)
            psA = [stkA.enter_context(nc.psum_tensor(f"psA{i}", [128, 512], F32))
                   for i in range(8)]
            (ps_proj, ps_xh, ps_y, ps_state, ps_g, ps_bc, ps_misc,
             ps_stats) = psA

            with ExitStack() as stg:
                wsA = [stg.enter_context(
                    nc.sbuf_tensor(f"wstgA{i}", [128, DP], F32))
                    for i in range(2)]
                items = []
                for k in range(NKT + 1):
                    nr = 128 if k < NKT else 1
                    items.append((wA[k] if k < NKT else wAb, nr, DP,
                                  winT.ap()[k * 128:k * 128 + nr, :]))
                for k in range(17):
                    nr = 128 if k < 16 else 1
                    items.append((wO[k] if k < 16 else wOb, nr, D,
                                  woutT.ap()[k * 128:k * 128 + nr, :]))
                casts = {}
                for i, (dst, nr, wd, src) in enumerate(items):
                    buf = wsA[i % 2]
                    tl = sch.op("sync", "dma_start", out=buf[0:nr, 0:wd],
                                in_=src,
                                waits=[casts[i - 2]] if i >= 2 else [],
                                inc="ld2" if i % 2 == 0 else "ld3", amt=16)
                    casts[i] = sch.op("vector", "tensor_copy",
                                      out=dst[0:nr, :], in_=buf[0:nr, 0:wd],
                                      waits=[tl], inc="vec")
                n = len(items)
                wA_ready = tmax([casts[n - 1]], [casts[n - 2]])

            t_dc = sch.op("tensor", "matmul", ps_misc[:, 256:256 + H],
                          onesr_f[:, :], drow[:, :], start=True, stop=True,
                          waits=tmax(tc, [t_onesr]), inc="mm")
            t_dc = sch.op("scalar", "copy", out=dcol[:],
                          in_=ps_misc[:, 256:256 + H], waits=[t_dc], inc="act")
            sch.use("ps_misc", t_dc)
            t_idh = []
            for h in range(H):
                t_idh.append(sch.op(
                    "vector", "tensor_scalar",
                    out=idh[:, 128 * h:128 * (h + 1)], in0=idn[:],
                    scalar1=dcol[:, h:h + 1], scalar2=None, op0=OP.mult,
                    waits=tmax([t_dc], tc), inc="vec"))
            t_idh = tmax(*t_idh)

            u_g = sbA("u_g", [128, NKT * TG])
            xnc = [sbA(f"xnc{i}", [128, TG], BF16) for i in range(2)]
            un_bf = sbA("un_bf", [128, NKT * TG], BF16)
            usq = sbA("usq", [128, TG])
            rowst = sbA("rowst", [1, 512])
            rowm = sbA("rowm", [1, TG])
            rowr = sbA("rowr", [1, TG])
            rowb = sbA("rowb", [1, TG])
            abc = sbA("abc", [128, TG])
            bbc = sbA("bbc", [128, TG])
            z_silu = sbA("z_silu", [128, 16 * TG], BF16)
            xh_sb = [sbA(f"xh{c}", [128, DI], BF16) for c in range(NCH)]
            btok = [sbA(f"btok{c}", [128, 2 * DS * 2], BF16) for c in range(NCH)]
            bfeat = sbA("bfeat", [64, H * TG], BF16)
            ctil = sbA("ctil", [64, H * TG], BF16)
            dtw = sbA("dtw", [1, H * TG])
            stw = sbA("stw", [1, H * TG])
            cumw = sbA("cumw", [1, H * TG])
            erow = sbA("erow", [1, H * TG])
            dtp4 = sbA("dtp4", [H, TG])
            wrow = sbA("wrow", [1, H * TG])
            wcol = sbA("wcol", [128, H * NCH])
            lam = sbA("lam", [128, H * NCH])
            lamrow = sbA("lamrow", [1, H])
            ebc = sbA("ebc", [128, Q])
            w2t = sbA("w2t", [128, 128])
            m2 = [sbA(f"m2_{i}", [128, 128], BF16) for i in range(2)]
            wbbf = sbA("wbbf", [128, DS], BF16)
            hA = sbA("hA", [64, H * HD])
            hB = sbA("hB", [64, H * HD])
            hbf = sbA("hbf", [64, H * HD], BF16)
            s_sb = [sbA(f"s_sb{i}", [128, TG]) for i in range(2)]

            h_cur, h_nxt = hA, hB
            t_h = None
            t_hbf = None

            for b in range(BLOC):
                for g in range(NG):
                    t_lu = sch.op("sync", "dma_start", out=u_g[:],
                                  in_=dslab(uT, b, g),
                                  waits=tmax(sch.guard("u_g"), wA_ready),
                                  inc="ld", amt=16)

                    t_stU = t_stQ = None
                    for k in range(NKT):
                        ks = slice(k * TG, (k + 1) * TG)
                        tq = sch.op("scalar", "activation", out=usq[:],
                                    in_=u_g[:, ks], func=AF.Square,
                                    waits=tmax([t_lu], sch.guard("usq")),
                                    inc="act")
                        t_stU = sch.op("tensor", "matmul", ps_stats[0:1, 0:TG],
                                       ones_f[:, :], u_g[:, ks],
                                       start=(k == 0), stop=(k == NKT - 1),
                                       waits=tmax([t_lu], [t_ones],
                                                  sch.guard("ps_stats_u") if k == 0 else []),
                                       inc="mm")
                        sch.use("u_g", t_stU)
                        t_stQ = sch.op("tensor", "matmul",
                                       ps_state[0:1, 0:TG],
                                       ones_f[:, :], usq[:],
                                       start=(k == 0), stop=(k == NKT - 1),
                                       waits=tmax([tq],
                                                  sch.guard("ps_state") if k == 0 else []),
                                       inc="mm")
                        sch.use("usq", t_stQ)
                    t_r0 = sch.op("vector", "tensor_copy", out=rowst[:, 0:TG],
                                  in_=ps_stats[0:1, 0:TG],
                                  waits=[t_stU], inc="vec")
                    sch.use("ps_stats_u", t_r0)
                    t_r0b = sch.op("vector", "tensor_copy",
                                   out=rowst[:, 256:256 + TG],
                                   in_=ps_state[0:1, 0:TG],
                                   waits=[t_stQ], inc="vec")
                    sch.use("ps_state", t_r0b)
                    t_r0 = tmax([t_r0], [t_r0b])
                    t_m = sch.op("vector", "tensor_single_scalar", out=rowm[:],
                                 in_=rowst[:, 0:TG], scalar=1.0 / D, op=OP.mult,
                                 waits=t_r0, inc="vec")
                    t_v = sch.op("vector", "tensor_scalar", out=rowr[:],
                                 in0=rowst[:, 256:256 + TG], scalar1=1.0 / D,
                                 scalar2=LN_EPS, op0=OP.mult, op1=OP.add,
                                 waits=t_r0, inc="vec")
                    t_m2 = sch.op("vector", "tensor_tensor", out=rowb[:],
                                  in0=rowm[:], in1=rowm[:], op=OP.mult,
                                  waits=[t_m], inc="vec")
                    t_v = sch.op("vector", "tensor_tensor", out=rowr[:],
                                 in0=rowr[:], in1=rowb[:], op=OP.subtract,
                                 waits=[t_v, t_m2], inc="vec")
                    t_v = sch.op("scalar", "activation", out=rowr[:],
                                 in_=rowr[:], func=AF.Ln, waits=[t_v], inc="act")
                    t_v = sch.op("scalar", "activation", out=rowr[:],
                                 in_=rowr[:], func=AF.Exp, scale=-0.5,
                                 waits=[t_v], inc="act")
                    t_bb = sch.op("vector", "tensor_tensor", out=rowb[:],
                                  in0=rowm[:], in1=rowr[:], op=OP.mult,
                                  waits=[t_v, t_m], inc="vec")
                    t_bb = sch.op("vector", "tensor_single_scalar", out=rowb[:],
                                  in_=rowb[:], scalar=-1.0, op=OP.mult,
                                  waits=[t_bb], inc="vec")
                    t_abm = sch.op("tensor", "matmul", ps_bc[:, 0:TG],
                                   onesr_f[:, :], rowr[:, :], start=True,
                                   stop=True,
                                   waits=tmax([t_v], [t_onesr],
                                              sch.guard("ps_bc")),
                                   inc="mm")
                    t_bbm = sch.op("tensor", "matmul", ps_bc[:, 256:256 + TG],
                                   onesr_f[:, :], rowb[:, :], start=True,
                                   stop=True, waits=[t_bb], inc="mm")
                    t_ab = sch.op("scalar", "copy", out=abc[:],
                                  in_=ps_bc[:, 0:TG],
                                  waits=tmax([t_abm, t_bbm],
                                             sch.guard("abc")), inc="act")
                    t_bbm = sch.op("scalar", "copy", out=bbc[:],
                                   in_=ps_bc[:, 256:256 + TG],
                                   waits=tmax([t_bbm],
                                              sch.guard("bbc")), inc="act")
                    sch.use("ps_bc", t_ab)
                    sch.use("ps_bc", t_bbm)
                    t_xn = []
                    for k in range(NKT):
                        ks = slice(k * TG, (k + 1) * TG)
                        tt1 = sch.op("vector", "tensor_tensor", out=u_g[:, ks],
                                     in0=u_g[:, ks], in1=abc[:], op=OP.mult,
                                     waits=tmax([t_ab], [t_lu]), inc="vec")
                        tt1 = sch.op("vector", "tensor_tensor", out=u_g[:, ks],
                                     in0=u_g[:, ks], in1=bbc[:], op=OP.add,
                                     waits=[tt1, t_bbm], inc="vec")
                        sch.use("u_g", tt1)
                        sch.use("abc", tt1)
                        sch.use("bbc", tt1)
                        t_xn.append(tt1)
                    t_xn = tmax(*t_xn)
                    for k in range(NKT):
                        ks = slice(k * TG, (k + 1) * TG)
                        t_xc = sch.op("vector", "tensor_copy",
                                      out=xnc[k % 2][:], in_=u_g[:, ks],
                                      waits=tmax(t_xn,
                                                 sch.guard(f"xnc{k % 2}")),
                                      inc="vec")
                        sch.use("u_g", t_xc)
                        t_xs = sch.op("gpsimd", "dma_start",
                                      out=dslab(xnbf_s, b, g, ktile=k),
                                      in_=xnc[k % 2][:], waits=[t_xc],
                                      inc="st", amt=16)
                        sch.use(f"xnc{k % 2}", t_xs)

                    t_rs = None
                    for k in range(NKT):
                        ks = slice(k * TG, (k + 1) * TG)
                        tq = sch.op("scalar", "activation", out=usq[:],
                                    in_=u_g[:, ks], func=AF.Square,
                                    waits=tmax(t_xn, sch.guard("usq")),
                                    inc="act")
                        sch.use("u_g", tq)
                        t_rs = sch.op("tensor", "matmul", ps_stats[0:1, 0:TG],
                                      ones_f[:, :], usq[:], start=(k == 0),
                                      stop=(k == NKT - 1),
                                      waits=tmax([tq], t_r0,
                                                 sch.guard("ps_stats_u") if k == 0 else []),
                                      inc="mm")
                        sch.use("usq", t_rs)
                    t_r2 = sch.op("vector", "tensor_scalar", out=rowm[:],
                                  in0=ps_stats[0:1, 0:TG], scalar1=1.0 / D,
                                  scalar2=RMS_EPS, op0=OP.mult, op1=OP.add,
                                  waits=[t_rs], inc="vec")
                    sch.use("ps_stats_u", t_r2)
                    t_r2 = sch.op("scalar", "activation", out=rowm[:],
                                  in_=rowm[:], func=AF.Ln, waits=[t_r2],
                                  inc="act")
                    t_r2 = sch.op("scalar", "activation", out=rowm[:],
                                  in_=rowm[:], func=AF.Exp, scale=-0.5,
                                  waits=[t_r2], inc="act")
                    t_r2 = sch.op("tensor", "matmul", ps_bc[:, 0:TG],
                                  onesr_f[:, :], rowm[:, :], start=True,
                                  stop=True,
                                  waits=tmax([t_r2], sch.guard("ps_bc")),
                                  inc="mm")
                    t_r2 = sch.op("scalar", "copy", out=abc[:],
                                  in_=ps_bc[:, 0:TG],
                                  waits=tmax([t_r2], sch.guard("abc")),
                                  inc="act")
                    sch.use("ps_bc", t_r2)
                    gun = sch.guard("un_bf")
                    t_un = []
                    for k in range(NKT):
                        ks = slice(k * TG, (k + 1) * TG)
                        tu = sch.op("vector", "scalar_tensor_tensor",
                                    out=un_bf[:, ks], in0=u_g[:, ks],
                                    scalar=normw[:, k:k + 1], in1=abc[:],
                                    op0=OP.mult, op1=OP.mult,
                                    waits=tmax([t_r2], t_xn, gun), inc="vec")
                        sch.use("u_g", tu)
                        sch.use("abc", tu)
                        t_un.append(tu)
                    t_un = tmax(*t_un)

                    t_mm = None
                    for k in range(NKT):
                        t_mm = sch.op("tensor", "matmul", ps_proj[0:H, 0:TG],
                                      wA[k][:, DTOFF:DTOFF + H],
                                      un_bf[:, k * TG:(k + 1) * TG],
                                      start=(k == 0), stop=False,
                                      waits=tmax(t_un, wA_ready,
                                                 sch.guard("ps_proj") if k == 0 else []),
                                      inc="mm")
                    t_mm = sch.op("tensor", "matmul", ps_proj[0:H, 0:TG],
                                  wAb[:, DTOFF:DTOFF + H], onesr_bf[:, 0:TG],
                                  start=False, stop=True,
                                  waits=[t_mm, t_onesbf], inc="mm")
                    t_dt = sch.op("scalar", "activation", out=dtp4[:],
                                  in_=ps_proj[0:H, 0:TG], func=AF.Exp,
                                  bias=dtb[:, 0:1], scale=1.0,
                                  waits=tmax([t_mm], tc, sch.guard("dtp4")),
                                  inc="act")
                    sch.use("ps_proj", t_dt)
                    t_dt = sch.op("scalar", "add", out=dtp4[:], in_=dtp4[:],
                                  add=1.0, waits=[t_dt], inc="act")
                    t_dt = sch.op("scalar", "activation", out=dtp4[:],
                                  in_=dtp4[:], func=AF.Ln, waits=[t_dt],
                                  inc="act")
                    t_dtw = sch.op("gpsimd", "dma_start",
                                   out=bass.AP(dtw, 0,
                                               [[H * TG, 1], [TG, H], [1, TG]]),
                                   in_=dtp4[:],
                                   waits=tmax([t_dt], sch.guard("dtw")),
                                   inc="sb", amt=16)
                    t_stp = sch.op("vector", "tensor_scalar", out=dtp4[:],
                                   in0=dtp4[:], scalar1=acol[:, 0:1],
                                   scalar2=None, op0=OP.mult,
                                   waits=tmax([t_dt], [t_acol], [t_dtw]),
                                   inc="vec")
                    sch.use("dtp4", t_stp)
                    t_stw = sch.op("gpsimd", "dma_start",
                                   out=bass.AP(stw, 0,
                                               [[H * TG, 1], [TG, H], [1, TG]]),
                                   in_=dtp4[:],
                                   waits=tmax([t_stp], sch.guard("stw")),
                                   inc="sb", amt=16)
                    sch.use("dtp4", t_stw)
                    t_cum = []
                    gcum = sch.guard("cumw")
                    for h in range(H):
                        for c in range(NCH):
                            off = h * TG + c * Q
                            t_cum.append(sch.op(
                                "vector", "tensor_tensor_scan",
                                out=cumw[:, off:off + Q],
                                data0=stw[:, off:off + Q], data1=zrow[:, 0:Q],
                                initial=0.0, op0=OP.add, op1=OP.add,
                                waits=tmax([t_stw], [t_zrow], gcum),
                                inc="vec"))
                            sch.use("stw", t_cum[-1])
                    t_cum = tmax(*t_cum)
                    t_er = sch.op("scalar", "activation", out=erow[:],
                                  in_=cumw[:], func=AF.Exp,
                                  waits=tmax(t_cum, sch.guard("erow")),
                                  inc="act")
                    t_wr = sch.op("scalar", "activation", out=wrow[:],
                                  in_=cumw[:], func=AF.Exp, scale=-1.0,
                                  waits=tmax(t_cum, sch.guard("wrow")),
                                  inc="act")
                    sch.use("cumw", t_er)
                    sch.use("cumw", t_wr)
                    t_wr = sch.op("vector", "tensor_tensor", out=wrow[:],
                                  in0=wrow[:], in1=dtw[:], op=OP.mult,
                                  waits=[t_wr, t_dtw], inc="vec")
                    sch.use("dtw", t_wr)
                    t_wc = []
                    t_lm = []
                    gw = sch.guard("wcol")
                    gl = sch.guard("lam")
                    for c in range(NCH):
                        for h in range(H):
                            off = h * TG + c * Q
                            tt1 = sch.op("tensor", "transpose",
                                         ps_misc[0:Q, 128:129],
                                         wrow[:, off:off + Q], idn[0:1, 0:1],
                                         waits=tmax([t_wr],
                                                    sch.guard("ps_misc")),
                                         inc="mm")
                            tt2 = sch.op("vector", "tensor_copy",
                                         out=wcol[:, c * H + h:c * H + h + 1],
                                         in_=ps_misc[0:Q, 128:129],
                                         waits=tmax([tt1], gw), inc="vec")
                            sch.use("ps_misc", tt2)
                            sch.use("wrow", tt2)
                            t_wc.append(tt2)
                        tt3v = sch.op("vector", "tensor_copy",
                                      out=lamrow[0:1, 0:H],
                                      in_=bass.AP(erow, c * Q + Q - 1,
                                                  [[H * TG, 1], [TG, H]]),
                                      waits=tmax([t_er],
                                                 sch.guard("lamrow")),
                                      inc="vec")
                        sch.use("erow", tt3v)
                        tt3 = sch.op("tensor", "matmul",
                                     ps_misc[:, 136:136 + H], onesr_f[:, :],
                                     lamrow[0:1, 0:H],
                                     start=True, stop=True,
                                     waits=tmax([tt3v], [t_onesr],
                                                sch.guard("ps_misc")),
                                     inc="mm")
                        tt4 = sch.op("scalar", "copy",
                                     out=lam[:, c * H:(c + 1) * H],
                                     in_=ps_misc[:, 136:136 + H],
                                     waits=tmax([tt3], gl), inc="act")
                        sch.use("ps_misc", tt4)
                        sch.use("lamrow", tt4)
                        t_lm.append(tt4)
                    t_wc = tmax(*t_wc)
                    t_lm = tmax(*t_lm)

                    t_bf = []
                    gbf = sch.guard("bfeat")
                    for hh in range(H):
                        pb, pnm = ((ps_proj, "ps_proj") if hh % 2 == 0
                                   else (ps_bc, "ps_bc"))
                        col0 = BOFF + hh * 64
                        t_mm = None
                        for k in range(NKT):
                            t_mm = sch.op("tensor", "matmul",
                                          pb[0:64, 0:TG],
                                          wA[k][:, col0:col0 + 64],
                                          un_bf[:, k * TG:(k + 1) * TG],
                                          start=(k == 0), stop=False,
                                          waits=tmax(t_un,
                                                     sch.guard(pnm) if k == 0 else []),
                                          inc="mm")
                        t_mm = sch.op("tensor", "matmul", pb[0:64, 0:TG],
                                      wAb[:, col0:col0 + 64],
                                      onesr_bf[:, 0:TG], start=False,
                                      stop=True, waits=[t_mm], inc="mm")
                        te = sch.op("scalar", "copy",
                                    out=bfeat[:, hh * TG:(hh + 1) * TG],
                                    in_=pb[0:64, 0:TG],
                                    waits=tmax([t_mm], gbf), inc="act")
                        sch.use(pnm, te)
                        t_bf.append(te)
                    t_bf = tmax(*t_bf)

                    t_ct = []
                    gct = sch.guard("ctil")
                    for hh in range(H):
                        pb, pnm = ((ps_proj, "ps_proj") if hh % 2 == 0
                                   else (ps_bc, "ps_bc"))
                        col0 = COFF + hh * 64
                        t_mm = None
                        for k in range(NKT):
                            t_mm = sch.op("tensor", "matmul",
                                          pb[0:64, 0:TG],
                                          wA[k][:, col0:col0 + 64],
                                          un_bf[:, k * TG:(k + 1) * TG],
                                          start=(k == 0), stop=False,
                                          waits=tmax(t_un,
                                                     sch.guard(pnm) if k == 0 else []),
                                          inc="mm")
                        t_mm = sch.op("tensor", "matmul", pb[0:64, 0:TG],
                                      wAb[:, col0:col0 + 64],
                                      onesr_bf[:, 0:TG], start=False,
                                      stop=True, waits=[t_mm], inc="mm")
                        for c in range(NCH):
                            e0 = hh * TG + c * Q
                            tb1 = sch.op("tensor", "matmul", ps_misc[0:64, 0:Q],
                                         onesr_f[:, 0:64],
                                         erow[:, e0:e0 + Q], start=True,
                                         stop=True,
                                         waits=tmax([t_er],
                                                    sch.guard("ps_misc")),
                                         inc="mm")
                            tb3 = sch.op("scalar", "copy", out=ebc[0:64, :],
                                         in_=ps_misc[0:64, 0:Q],
                                         waits=tmax([tb1], sch.guard("ebc")),
                                         inc="act")
                            sch.use("ps_misc", tb3)
                            sch.use("erow", tb3)
                            tcv = sch.op("vector", "tensor_tensor",
                                         out=ctil[:, hh * TG + c * Q:hh * TG + c * Q + Q],
                                         in0=pb[0:64, c * Q:c * Q + Q],
                                         in1=ebc[0:64, :], op=OP.mult,
                                         waits=tmax([t_mm, tb3], gct),
                                         inc="vec")
                            sch.use("ebc", tcv)
                            sch.use(pnm, tcv)
                            t_ct.append(tcv)
                    t_ct = tmax(*t_ct)

                    t_zs = []
                    gz = sch.guard("z_silu")
                    for i in range(16):
                        pb, pnm = ((ps_proj, "ps_proj") if i % 2 == 0
                                   else (ps_bc, "ps_bc"))
                        col0 = ZOFF + i * 128
                        t_mm = None
                        for k in range(NKT):
                            t_mm = sch.op("tensor", "matmul",
                                          pb[:, 0:TG],
                                          wA[k][:, col0:col0 + 128],
                                          un_bf[:, k * TG:(k + 1) * TG],
                                          start=(k == 0), stop=False,
                                          waits=tmax(t_un,
                                                     sch.guard(pnm) if k == 0 else []),
                                          inc="mm")
                        t_mm = sch.op("tensor", "matmul",
                                      pb[:, 0:TG],
                                      wAb[:, col0:col0 + 128],
                                      onesr_bf[:, 0:TG], start=False,
                                      stop=True, waits=[t_mm], inc="mm")
                        te = sch.op("scalar", "activation",
                                    out=z_silu[:, i * TG:(i + 1) * TG],
                                    in_=pb[:, 0:TG],
                                    func=AF.Silu, waits=tmax([t_mm], gz),
                                    inc="act")
                        sch.use(pnm, te)
                        t_zs.append(te)

                    t_xh = {}
                    t_bt = {}
                    rr = 0
                    for c in range(NCH):
                        evs = []
                        for psl in range(4):
                            pb, pnm = ((ps_xh, "ps_xh") if rr % 2 == 0
                                       else (ps_y, "ps_y"))
                            rr += 1
                            t_mm = None
                            for k in range(NKT):
                                t_mm = sch.op(
                                    "tensor", "matmul", pb[:, 0:512],
                                    un_bf[:, k * TG + c * Q:k * TG + c * Q + Q],
                                    wA[k][:, XOFF + psl * 512:XOFF + (psl + 1) * 512],
                                    start=(k == 0), stop=False,
                                    waits=tmax(t_un,
                                               sch.guard(pnm) if k == 0 else []),
                                    inc="mm")
                            t_mm = sch.op("tensor", "matmul", pb[:, 0:512],
                                          onesr_bf[:, 0:Q],
                                          wAb[:, XOFF + psl * 512:XOFF + (psl + 1) * 512],
                                          start=False, stop=True,
                                          waits=[t_mm], inc="mm")
                            te = sch.op("scalar", "copy",
                                        out=xh_sb[c][:, psl * 512:(psl + 1) * 512],
                                        in_=pb[:, 0:512],
                                        waits=tmax([t_mm],
                                                   sch.guard(f"xh{c}_{psl}")),
                                        inc="act")
                            sch.use(pnm, te)
                            evs.append(te)
                        t_xh[c] = tmax(*evs)
                        pb, pnm = ((ps_xh, "ps_xh") if rr % 2 == 0
                                   else (ps_y, "ps_y"))
                        rr += 1
                        t_mm = None
                        for k in range(NKT):
                            t_mm = sch.op(
                                "tensor", "matmul", pb[:, 0:256],
                                un_bf[:, k * TG + c * Q:k * TG + c * Q + Q],
                                wA[k][:, BOFF:BOFF + 256],
                                start=(k == 0), stop=False,
                                waits=tmax(t_un,
                                           sch.guard(pnm) if k == 0 else []),
                                inc="mm")
                        t_mm = sch.op("tensor", "matmul", pb[:, 0:256],
                                      onesr_bf[:, 0:Q], wAb[:, BOFF:BOFF + 256],
                                      start=False, stop=True, waits=[t_mm],
                                      inc="mm")
                        t_bt[c] = sch.op("scalar", "copy", out=btok[c][:],
                                         in_=pb[:, 0:256],
                                         waits=tmax([t_mm],
                                                    sch.guard(f"btok{c}")),
                                         inc="act")
                        sch.use(pnm, t_bt[c])

                    sch.use("un_bf", t_bt[NCH - 1])
                    sch.use("un_bf", t_zs[-1])

                    for c in range(NCH):
                        first = (g == 0 and c == 0)
                        y_evacs = []
                        for h in range(H):
                            ccols = slice(h * TG + c * Q, h * TG + c * Q + Q)
                            t_w2 = sch.op("vector", "tensor_scalar",
                                          out=w2t[:], in0=tri[:],
                                          scalar1=wcol[:, c * H + h:c * H + h + 1],
                                          scalar2=None, op0=OP.mult,
                                          waits=tmax(t_wc, tc,
                                                     sch.guard("w2t")),
                                          inc="vec")
                            sch.use("wcol", t_w2)
                            t_gmm = sch.op("tensor", "matmul", ps_g[:, 0:Q],
                                           bfeat[0:64, ccols],
                                           ctil[0:64, ccols],
                                           start=True, stop=True,
                                           waits=tmax(t_bf, t_ct,
                                                      sch.guard("ps_g")),
                                           inc="mm")
                            t_m2 = sch.op("vector", "scalar_tensor_tensor",
                                          out=m2[h % 2][:], in0=ps_g[:, 0:Q],
                                          scalar=1.0, in1=w2t[:],
                                          op0=OP.bypass, op1=OP.mult,
                                          waits=tmax([t_gmm, t_w2],
                                                     sch.guard(f"m2_{h % 2}")),
                                          inc="vec")
                            sch.use("ps_g", t_m2)
                            sch.use("w2t", t_m2)
                            tprev = sch.guard("ps_y")
                            last_y = None
                            for psl in range(4):
                                pc = psl * Q
                                xsl = slice(h * HD + psl * Q, h * HD + (psl + 1) * Q)
                                tm1 = sch.op("tensor", "matmul",
                                             ps_y[:, pc:pc + Q],
                                             xh_sb[c][:, xsl], m2[h % 2][:],
                                             start=True, stop=False,
                                             waits=tmax([t_m2], [t_xh[c]],
                                                        tprev),
                                             inc="mm")
                                tm2 = sch.op("tensor", "matmul",
                                             ps_y[:, pc:pc + Q],
                                             xh_sb[c][:, xsl],
                                             idh[:, 128 * h:128 * (h + 1)],
                                             start=False, stop=first,
                                             waits=tmax([tm1], t_idh),
                                             inc="mm")
                                last_y = tm2
                                if not first:
                                    last_y = sch.op(
                                        "tensor", "matmul", ps_y[:, pc:pc + Q],
                                        hbf[0:64, xsl],
                                        ctil[0:64, ccols],
                                        start=False, stop=True,
                                        waits=tmax([tm2], [t_hbf]),
                                        inc="mm")
                                sch.use(f"m2_{h % 2}", last_y)
                                sch.use("hbf", last_y)
                            for psl in range(4):
                                zt = 4 * h + psl
                                gsl = slice(zt * TG + c * Q, zt * TG + c * Q + Q)
                                ge = sch.op("vector", "tensor_tensor",
                                            out=z_silu[:, gsl],
                                            in0=ps_y[:, psl * Q:(psl + 1) * Q],
                                            in1=z_silu[:, gsl], op=OP.mult,
                                            waits=tmax([last_y], [t_zs[zt]]),
                                            inc="vec")
                                sch.use("ps_y", ge)
                                sch.use("z_silu", ge)
                                y_evacs.append(ge)
                            t_wb = sch.op("vector", "tensor_scalar",
                                          out=wbbf[:],
                                          in0=btok[c][:, h * DS:(h + 1) * DS],
                                          scalar1=wcol[:, c * H + h:c * H + h + 1],
                                          scalar2=lam[:, c * H + h:c * H + h + 1],
                                          op0=OP.mult, op1=OP.mult,
                                          waits=tmax([t_bt[c]], t_wc, t_lm,
                                                     sch.guard("wbbf")),
                                          inc="vec")
                            sch.use("wcol", t_wb)
                            sch.use("lam", t_wb)
                            t_smm = sch.op("tensor", "matmul",
                                           ps_state[0:64, 0:512], wbbf[:],
                                           xh_sb[c][:, h * HD:(h + 1) * HD],
                                           start=True, stop=True,
                                           waits=tmax([t_wb], [t_xh[c]],
                                                      sch.guard("ps_state")),
                                           inc="mm")
                            sch.use("wbbf", t_smm)
                            hsl = slice(h * HD, (h + 1) * HD)
                            if first:
                                t_hn = sch.op("scalar", "copy",
                                              out=h_nxt[0:64, hsl],
                                              in_=ps_state[0:64, 0:512],
                                              waits=[t_smm], inc="act")
                            else:
                                t_hn = sch.op("vector", "scalar_tensor_tensor",
                                              out=h_nxt[0:64, hsl],
                                              in0=h_cur[0:64, hsl],
                                              scalar=lam[0:64, c * H + h:c * H + h + 1],
                                              in1=ps_state[0:64, 0:512],
                                              op0=OP.mult, op1=OP.add,
                                              waits=tmax([t_smm], [t_h] if t_h else [],
                                                         t_lm),
                                              inc="vec")
                                sch.use("lam", t_hn)
                            sch.use("ps_state", t_hn)
                            sch.use(f"xh{c}_{h}", t_hn)
                            sch.use("h_nxt", t_hn)
                        ghbf = sch.guard("hbf")
                        t_hbf = sch.op("vector", "tensor_copy", out=hbf[:],
                                       in_=h_nxt[:],
                                       waits=tmax(sch.guard("h_nxt"), ghbf),
                                       inc="vec")
                        t_h = t_hbf
                        h_cur, h_nxt = h_nxt, h_cur
                        for psl in range(4):
                            sch.use(f"xh{c}_{psl}", t_hbf)
                        sch.use(f"btok{c}", t_hbf)
                        t_y_all = tmax(*y_evacs)
                        if c == NCH - 1:
                            t_y_group = t_y_all
                    t_slast = None
                    for dti in range(NKT):
                        pb, pnm = ((ps_proj, "ps_proj") if dti % 2 == 0
                                   else (ps_bc, "ps_bc"))
                        t_mm = None
                        for pt in range(16):
                            t_mm = sch.op("tensor", "matmul", pb[:, 0:TG],
                                          wO[pt][:, dti * 128:(dti + 1) * 128],
                                          z_silu[:, pt * TG:(pt + 1) * TG],
                                          start=(pt == 0), stop=False,
                                          waits=tmax(t_y_group,
                                                     sch.guard(pnm) if pt == 0 else []),
                                          inc="mm")
                        t_mm = sch.op("tensor", "matmul", pb[:, 0:TG],
                                      wOb[:, dti * 128:(dti + 1) * 128],
                                      onesr_bf[:, 0:TG], start=False,
                                      stop=True, waits=[t_mm], inc="mm")
                        t_ev = sch.op("scalar", "copy",
                                      out=s_sb[dti % 2][:],
                                      in_=pb[:, 0:TG],
                                      waits=tmax([t_mm],
                                                 sch.guard(f"s_sb{dti % 2}")),
                                      inc="act")
                        sch.use(pnm, t_ev)
                        t_sst = sch.op("gpsimd", "dma_start",
                                       out=dslab(s_s, b, g, ktile=dti),
                                       in_=s_sb[dti % 2][:], waits=[t_ev],
                                       inc="st", amt=16)
                        sch.use(f"s_sb{dti % 2}", t_sst)
                        if dti == NKT - 1:
                            sch.use("z_silu", t_ev)

            phaseA_end = sch.snap()

        sch.barrier_all()
        with ExitStack() as stkB:
            def sbB(n, s, dt=F32):
                return stkB.enter_context(nc.sbuf_tensor(n, s, dt))

            w1 = [sbB(f"w1_{k}", [128, FF], BF16) for k in range(NKT)]
            w1b = sbB("w1b", [1, FF], BF16)
            w2 = [sbB(f"w2_{k}", [128, D], BF16) for k in range(32)]
            w2b = sbB("w2b", [1, D], BF16)
            ps_q = stkB.enter_context(nc.psum_tensor("ps_q", [128, 1024], F32))
            ps_o = stkB.enter_context(nc.psum_tensor("ps_o", [128, 512], F32))
            ps_h = stkB.enter_context(nc.psum_tensor("ps_h", [128, 1024], F32))

            xnbf_g = sbB("xnbf_g", [128, NKT * TG], BF16)
            uc = sbB("uc", [128, 1024])
            tmpr = [sbB(f"tmpr{i}", [128, 1024]) for i in range(3)]
            y3 = sbB("y3", [128, 2 * TG])
            gq = sbB("gq", [128, 2 * TG])
            s_t = sbB("s_t", [128, 2 * TG])
            u_t = sbB("u_t", [128, 2 * TG])

            with ExitStack() as stg:
                wsB = [stg.enter_context(
                    nc.sbuf_tensor(f"wstgB{i}", [128, FF], F32))
                    for i in range(2)]
                items = []
                for k in range(NKT + 1):
                    nr = 128 if k < NKT else 1
                    items.append((w1[k] if k < NKT else w1b, nr, FF, None,
                                  w1a_d.ap()[k * 128:k * 128 + nr, :]))
                for k in range(33):
                    nr = 128 if k < 32 else 1
                    items.append((w2[k] if k < 32 else w2b, nr, D,
                                  1.0 / 256.0 if k < 32 else None,
                                  w2a_d.ap()[k * 128:k * 128 + nr, :]))
                casts = {}
                for i, (dst, nr, wd, scl, src) in enumerate(items):
                    buf = wsB[i % 2]
                    tl = sch.op("sync", "dma_start", out=buf[0:nr, 0:wd],
                                in_=src,
                                waits=[casts[i - 2]] if i >= 2 else [],
                                inc="ld2" if i % 2 == 0 else "ld3", amt=16)
                    if scl is None:
                        casts[i] = sch.op("vector", "tensor_copy",
                                          out=dst[0:nr, :],
                                          in_=buf[0:nr, 0:wd],
                                          waits=[tl], inc="vec")
                    else:
                        casts[i] = sch.op("vector", "tensor_scalar",
                                          out=dst[0:nr, :],
                                          in0=buf[0:nr, 0:wd],
                                          scalar1=scl, scalar2=None,
                                          op0=OP.mult, waits=[tl], inc="vec")
                n = len(items)
                w1_ready = tmax([casts[NKT]], [casts[NKT - 1]])
                w2_ready = tmax([casts[n - 1]], [casts[n - 2]])

            casA = sbB("casA", [128, 1024])
            casB = sbB("casB", [128, 1024])
            st1 = sbB("st1", [128, 1024])
            accA = sbB("accA", [128, 1024])
            accB = sbB("accB", [128, 1024])
            hq_sb = sbB("hq_sb", [128, 32 * TG], BF16)
            y3c, c3A, c3B, st3 = accA, casA, casB, st1
            EP = slice(0, 2 * TG)

            for b in range(BLOC):
                for g in range(NG):
                    t_lx = sch.op("sync", "dma_start", out=xnbf_g[:],
                                  in_=dslab(xnbf_s, b, g),
                                  waits=tmax(sch.guard("xnbf_g")),
                                  inc="ld", amt=16)
                    for qd in range(8):
                        gq_ps = sch.guard("ps_q")
                        t_mm = None
                        for j in range(4):
                            ft = 4 * qd + j
                            jc = slice(j * TG, (j + 1) * TG)
                            for k in range(NKT):
                                t_mm = sch.op(
                                    "tensor", "matmul", ps_q[:, jc],
                                    w1[k][:, ft * 128:(ft + 1) * 128],
                                    xnbf_g[:, k * TG:(k + 1) * TG],
                                    start=(k == 0), stop=False,
                                    waits=tmax([t_lx], w1_ready,
                                               gq_ps if (k == 0 and j == 0) else []),
                                    inc="mm")
                            t_mm = sch.op("tensor", "matmul", ps_q[:, jc],
                                          w1b[:, ft * 128:(ft + 1) * 128],
                                          onesr_bf[:, 0:TG], start=False,
                                          stop=True, waits=[t_mm, t_onesbf],
                                          inc="mm")
                        t_y5 = sch.op("scalar", "activation", out=uc[:],
                                      in_=ps_q[:, 0:1024], func=AF.Copy,
                                      bias=2048.5, scale=256.0,
                                      waits=tmax([t_mm], sch.guard("uc")),
                                      inc="act")
                        sch.use("ps_q", t_y5)
                        LE = ("vector", "vector")
                        CS = [slice(0, 1024)]
                        NLANE = 1
                        t_cl = {}
                        for ln in range(NLANE):
                            t_cl[ln] = sch.op(LE[ln], "tensor_scalar",
                                              out=uc[:, CS[ln]],
                                              in0=uc[:, CS[ln]], scalar1=0.5,
                                              scalar2=4095.5, op0=OP.max,
                                              op1=OP.min,
                                              waits=tmax([t_y5]),
                                              inc="vec")
                        t_yk = {}
                        for ln in range(NLANE):
                            cs = CS[ln]
                            sem = "vec"
                            src, dst = uc, casA
                            tk = t_cl[ln]
                            for bi in range(11, -1, -1):
                                pw = float(2 ** bi)
                                ts_ = sch.op(LE[ln], "tensor_scalar",
                                             out=st1[:, cs], in0=src[:, cs],
                                             scalar1=pw, scalar2=pw,
                                             op0=OP.is_ge, op1=OP.mult,
                                             waits=[tk], inc=sem)
                                tk = sch.op(LE[ln], "tensor_tensor",
                                            out=dst[:, cs], in0=src[:, cs],
                                            in1=st1[:, cs], op=OP.subtract,
                                            waits=[ts_], inc=sem)
                                src, dst = (casA, casB) if src is uc else \
                                           ((casB, casA) if src is casA else (casA, casB))
                            t_yk[ln] = sch.op(LE[ln], "tensor_tensor",
                                              out=dst[:, cs], in0=uc[:, cs],
                                              in1=src[:, cs], op=OP.subtract,
                                              waits=[tk], inc=sem)
                            yk_buf = dst
                        for ln in range(NLANE):
                            sch.use("uc", t_yk[ln])
                        t_relu = {}
                        t_mac = {0: None}
                        asrc, adst = accA, accB
                        gacc = tmax(sch.guard("accA"), sch.guard("accB"))
                        for j in range(NSEG):
                            wl = [t_yk[ln2] for ln2 in range(NLANE)]
                            if j >= 3:
                                wl.append(t_relu[j - 3][1])
                            t_r = sch.op("scalar", "activation",
                                         out=tmpr[j % 3][:], in_=yk_buf[:],
                                         func=AF.Relu, bias=knt[:, j:j + 1],
                                         scale=1.0, waits=tmax(*wl, tc),
                                         inc="act")
                            if j == 0:
                                tm = sch.op("vector", "tensor_scalar",
                                            out=adst[:], in0=tmpr[0][:],
                                            scalar1=GAM[0], scalar2=BASE2,
                                            op0=OP.mult, op1=OP.add,
                                            waits=tmax([t_r], gacc), inc="vec")
                            elif j == 1:
                                tm = sch.op("vector", "tensor_scalar",
                                            out=st1[:], in0=tmpr[1][:],
                                            scalar1=GAM[1], scalar2=Y0,
                                            op0=OP.mult, op1=OP.add,
                                            waits=tmax([t_r], gacc), inc="vec")
                                tm = sch.op("vector", "tensor_tensor",
                                            out=adst[:], in0=asrc[:],
                                            in1=st1[:], op=OP.add,
                                            waits=[tm, t_mac[0]], inc="vec")
                            else:
                                tm = sch.op("vector", "scalar_tensor_tensor",
                                            out=adst[:], in0=tmpr[j % 3][:],
                                            scalar=GAM[j], in1=asrc[:],
                                            op0=OP.mult, op1=OP.add,
                                            waits=[t_r, t_mac[0]], inc="vec")
                            t_mac[0] = tm
                            t_relu[j] = (t_r, tm)
                            asrc, adst = adst, asrc
                        t_hmm = t_mac[0]
                        oth = casA if yk_buf is casB else casB
                        t_hqf = {}
                        for ln in range(NLANE):
                            cs = CS[ln]
                            sem = "vec"
                            tk = t_hmm
                            src, dst = asrc, oth
                            for bi in range(11, -1, -1):
                                pw = float(2 ** bi)
                                ts_ = sch.op(LE[ln], "tensor_scalar",
                                             out=st1[:, cs], in0=src[:, cs],
                                             scalar1=pw, scalar2=pw,
                                             op0=OP.is_ge, op1=OP.mult,
                                             waits=[tk], inc=sem)
                                tk = sch.op(LE[ln], "tensor_tensor",
                                            out=dst[:, cs], in0=src[:, cs],
                                            in1=st1[:, cs], op=OP.subtract,
                                            waits=[ts_], inc=sem)
                                if src is asrc:
                                    src, dst = oth, yk_buf
                                else:
                                    src, dst = dst, src
                            t_hqi = sch.op(LE[ln], "tensor_tensor",
                                           out=st1[:, cs], in0=asrc[:, cs],
                                           in1=src[:, cs], op=OP.subtract,
                                           waits=[tk], inc=sem)
                            qcol = slice(qd * 1024 + cs.start,
                                         qd * 1024 + cs.stop)
                            t_hqf[ln] = sch.op(LE[ln], "tensor_scalar",
                                               out=hq_sb[:, qcol],
                                               in0=st1[:, cs], scalar1=-1024.0,
                                               scalar2=None, op0=OP.add,
                                               waits=tmax([t_hqi],
                                                          sch.guard(f"hq{qd}_{ln}")),
                                               inc=sem)
                        for ln in range(NLANE):
                            sch.use("accA", t_hqf[ln])
                            sch.use("accB", t_hqf[ln])
                            sch.use(f"hqtok{qd}", t_hqf[ln])
                    sch.use("xnbf_g", t_mm)
                    t_hq_all = [sch.guard(f"hqtok{qd}") for qd in range(8)]
                    t_hq_all = tmax(*t_hq_all)

                    for dp in range(NKT // 2):
                        t_mm = None
                        for half in range(2):
                            dti = 2 * dp + half
                            pcol = half * TG
                            for ft in range(32):
                                t_mm = sch.op(
                                    "tensor", "matmul",
                                    ps_o[:, pcol:pcol + TG],
                                    w2[ft][:, dti * 128:(dti + 1) * 128],
                                    hq_sb[:, ft * TG:(ft + 1) * TG],
                                    start=(ft == 0), stop=False,
                                    waits=tmax(t_hq_all, w2_ready,
                                               sch.guard("ps_o")
                                               if (ft == 0 and half == 0)
                                               else []),
                                    inc="mm")
                            t_mm = sch.op("tensor", "matmul",
                                          ps_o[:, pcol:pcol + TG],
                                          w2b[:, dti * 128:(dti + 1) * 128],
                                          onesr_bf[:, 0:TG], start=False,
                                          stop=True, waits=[t_mm], inc="mm")
                        for qd in range(8):
                            sch.use(f"hq{qd}_0", t_mm)
                            sch.use(f"hq{qd}_1", t_mm)
                        t_y3 = sch.op("scalar", "activation", out=y3[:],
                                      in_=ps_o[:, 0:2 * TG],
                                      func=AF.Copy, bias=512.5, scale=256.0,
                                      waits=tmax([t_mm], sch.guard("y3")),
                                      inc="act")
                        sch.use("ps_o", t_y3)
                        lne = "vector"
                        lsem = "vec"
                        t_c = sch.op(lne, "tensor_scalar", out=y3c[:, EP],
                                     in0=y3[:], scalar1=0.5, scalar2=1023.5,
                                     op0=OP.max, op1=OP.min,
                                     waits=tmax([t_y3]),
                                     inc=lsem)
                        sch.use("y3", t_c)
                        src, dst = y3c, c3A
                        tk = t_c
                        for bi in range(9, -1, -1):
                            pw = float(2 ** bi)
                            ts_ = sch.op(lne, "tensor_scalar", out=st3[:, EP],
                                         in0=src[:, EP], scalar1=pw,
                                         scalar2=pw,
                                         op0=OP.is_ge, op1=OP.mult,
                                         waits=[tk], inc=lsem)
                            tk = sch.op(lne, "tensor_tensor",
                                        out=dst[:, EP], in0=src[:, EP],
                                        in1=st3[:, EP],
                                        op=OP.subtract,
                                        waits=[ts_], inc=lsem)
                            src, dst = (c3A, c3B) if src is y3c else \
                                       ((c3B, c3A) if src is c3A else (c3A, c3B))
                        t_q2 = sch.op(lne, "tensor_tensor",
                                      out=dst[:, EP], in0=y3c[:, EP],
                                      in1=src[:, EP],
                                      op=OP.subtract,
                                      waits=[tk], inc=lsem)
                        fl_buf = dst
                        t_g = sch.op("scalar", "activation", out=gq[:],
                                     in_=fl_buf[:, EP], func=AF.Silu,
                                     bias=ccol[:, 0:1], scale=1.0 / 256.0,
                                     waits=tmax([t_q2], tc, sch.guard("gq")),
                                     inc="act")
                        t_ls = sch.guard("s_t")
                        t_lu2 = sch.guard("u_t")
                        for half in range(2):
                            hc = slice(half * TG, (half + 1) * TG)
                            t_ls = tmax([sch.op(
                                "sync", "dma_start", out=s_t[:, hc],
                                in_=dslab(s_s, b, g, ktile=2 * dp + half),
                                waits=t_ls, inc="ld2", amt=16)])
                            t_lu2 = tmax([sch.op(
                                "sync", "dma_start", out=u_t[:, hc],
                                in_=dslab(uT, b, g, ktile=2 * dp + half),
                                waits=t_lu2, inc="ld3", amt=16)])
                        t_o = sch.op("vector", "tensor_tensor", out=s_t[:],
                                     in0=s_t[:], in1=gq[:], op=OP.mult,
                                     waits=tmax([t_g], t_ls), inc="vec")
                        sch.use("gq", t_o)
                        t_o = sch.op("vector", "tensor_tensor", out=s_t[:],
                                     in0=s_t[:], in1=u_t[:], op=OP.add,
                                     waits=tmax([t_o], t_lu2), inc="vec")
                        sch.use("u_t", t_o)
                        for half in range(2):
                            hc = slice(half * TG, (half + 1) * TG)
                            t_st = sch.op("gpsimd", "dma_start",
                                          out=dslab(outT, b, g,
                                                    ktile=2 * dp + half),
                                          in_=s_t[:, hc], waits=[t_o],
                                          inc="st", amt=16)
                            sch.use("s_t", t_st)

            sch.wait_only("gpsimd", [("st", sch.count["st"]),
                                     ("st2", sch.count["st2"]),
                                     ("sb", sch.count["sb"])])

        with nc.Block() as block:
            @block.sync
            def _(eng):
                sch.replay("sync", eng, sems)

            @block.gpsimd
            def _(eng):
                sch.replay("gpsimd", eng, sems)

            @block.tensor
            def _(eng):
                sch.replay("tensor", eng, sems)

            @block.vector
            def _(eng):
                sch.replay("vector", eng, sems)

            @block.scalar
            def _(eng):
                sch.replay("scalar", eng, sems)

    return nc


_NC_CACHE = {}


def kernel(**inputs):
    u = np.asarray(inputs["L0_attn_u"], np.float32)
    Win = np.asarray(inputs["L0_attn_in_proj_w"], np.float32)
    bin_ = np.asarray(inputs["L0_attn_in_proj_b"], np.float32)
    normw = np.asarray(inputs["L0_attn_norm_w"], np.float32)
    alog = np.asarray(inputs["L0_attn_A_log"], np.float32)
    dd = np.asarray(inputs["L0_attn_D"], np.float32)
    dtb = np.asarray(inputs["L0_attn_dt_bias"], np.float32)
    Wout = np.asarray(inputs["L0_attn_out_proj_w"], np.float32)
    bout = np.asarray(inputs["L0_attn_out_proj_b"], np.float32)
    w1 = np.asarray(inputs["L0_ffn_w1"], np.float32)
    b1 = np.asarray(inputs["L0_ffn_b1"], np.float32)
    w2 = np.asarray(inputs["L0_ffn_w2"], np.float32)
    b2 = np.asarray(inputs["L0_ffn_b2"], np.float32)

    perm = np.concatenate(
        [np.arange(4100)]
        + [4100 + 128 * h + np.arange(64) for h in range(4)]
        + [4100 + 128 * h + 64 + np.arange(64) for h in range(4)])
    winT = np.ascontiguousarray(
        np.concatenate([Win[perm].T, bin_[perm][None, :]], 0), np.float32)
    woutT = np.ascontiguousarray(
        np.concatenate([Wout.T, bout[None, :]], 0), np.float32)
    w1a = np.ascontiguousarray(np.concatenate([w1, b1[None, :]], 0), np.float32)
    w2a = np.ascontiguousarray(np.concatenate([w2, b2[None, :]], 0), np.float32)
    tri = np.triu(np.ones((128, 128), np.float32))
    idn = np.eye(128, dtype=np.float32)
    knots = -(64.0 * np.arange(NSEG) + 1024.0)
    knt = np.ascontiguousarray(
        np.broadcast_to(knots[None, :], (128, NSEG)), np.float32)
    ccol = np.zeros((128, 4), np.float32)
    ccol[:, 0] = -2.0

    uT = np.ascontiguousarray(
        u.reshape(NCORE, BLOC, L, D).transpose(0, 1, 3, 2), np.float32)

    if "nc" not in _NC_CACHE:
        _NC_CACHE["nc"] = build_program()
    nc = _NC_CACHE["nc"]

    shared = dict(winT=winT, woutT=woutT, w1a=w1a, w2a=w2a, normw=normw,
                  alog=alog, dd=dd, dtb=dtb, tri=tri, idn=idn, knt=knt,
                  ccol=ccol)
    in_maps = [dict(uT=uT[i], **shared) for i in range(NCORE)]
    res = run_bass_kernel_spmd(nc, in_maps, core_ids=list(range(NCORE)))
    outs = [np.asarray(res.results[i]["outT"]) for i in range(NCORE)]
    out = np.concatenate(outs, 0).reshape(BFULL, D, L).transpose(0, 2, 1)
    return np.ascontiguousarray(out, np.float32)


if __name__ == "__main__":
    nc = build_program()
    n = sum(1 for _ in ())
    print("build ok")
